# version 20
# baseline (speedup 1.0000x reference)
import contextlib

import numpy as np

import concourse.bass as bass
import concourse.mybir as mybir
from concourse.bass_utils import run_bass_kernel_spmd

F32 = mybir.dt.float32
F32R = mybir.dt.float32r
BF16 = mybir.dt.bfloat16
Alu = mybir.AluOpType
Act = mybir.ActivationFunctionType
AxX = mybir.AxisListType.X

D = 1024
C = 8
TC = 2048
TT = 512
NT = TC // TT
NG = TT // 128
H = 16
DH = 64
E = 8
SIC = 16
EPS_LN1 = 1e-5
EPS_LN2 = 1e-3
EPS_QK = 1e-6
RSCALE = 2.5

ENGS = ("sync", "scalar", "vector", "tensor", "gpsimd")

SB_TOP = 224 * 1024 - 64
RA = SB_TOP - 64 * 1024
RB = RA - 32 * 1024
RC = RB - 32 * 1024

N_UP0 = 0
N_DN0 = 16
N_SU0 = 32
N_SD0 = 40
N_WV0 = 48
N_WV20 = 52
N_WO0 = 56
N_WSLAB = 60


class _Em:

    def __init__(self):
        self.prog = {e: [] for e in ENGS}
        self.cnt = {}
        self.hwm = {}

    def emit(self, eng, fn, waits=(), inc=None):
        wl = []
        for tok in waits:
            if tok is None:
                continue
            s, v = tok
            key = (eng, s.name)
            if self.hwm.get(key, -1) >= v:
                continue
            self.hwm[key] = v
            wl.append((s, v))
        tok = None
        if inc is not None:
            sem, amt = inc if isinstance(inc, tuple) else (inc, 1)
            self.cnt[sem.name] = self.cnt.get(sem.name, 0) + amt
            tok = (sem, self.cnt[sem.name])
            inc = (sem, amt)

        def run(engine, fn=fn, wl=wl, inc=inc):
            for s, v in wl:
                engine.wait_ge(s, v)
            inst = fn(engine)
            if inc is not None:
                inst.then_inc(inc[0], inc[1])

        self.prog[eng].append(run)
        return tok

    def wait(self, eng, *toks):
        wl = []
        for tok in toks:
            if tok is None:
                continue
            s, v = tok
            key = (eng, s.name)
            if self.hwm.get(key, -1) >= v:
                continue
            self.hwm[key] = v
            wl.append((s, v))
        if wl:
            def run(engine, wl=wl):
                for s, v in wl:
                    engine.wait_ge(s, v)
            self.prog[eng].append(run)


def build_program(cc_groups=((0, 1, 2, 3), (4, 5, 6, 7)), debug=False):
    nc = bass.Bass("TRN2", target_bir_lowering=False,
                   detect_race_conditions=False)

    di = {}
    def din(name, shape):
        di[name] = nc.dram_tensor(name, list(shape), F32, kind="ExternalInput").ap()
    din("x", (128, C, TC))
    din("w1", (16, 128, 2048))
    din("wffn", (N_WSLAB, 128, 2048))
    din("brows", (1, 7, 512))
    din("bcols", (128, 7, C))
    din("gatet", (128, C, E))
    din("gbias", (128, H, E))
    din("ident", (128, 128))
    din("hmask", (H, 2 * D))
    out_d = nc.dram_tensor("out", [128, C, TC], F32, kind="ExternalOutput").ap()
    dbg = {}
    if debug:
        for nm, shp, dt_ in (("d_sct", (128, H, E), F32), ("d_sfc", (128, H, E), F32),
                             ("d_selw", (128, H, E), F32), ("d_cwtm", (128, H, E), F32),
                             ("d_spk", (128, 512), F32), ("d_sfl", (128, 512), F32),
                             ("d_xn", (128, C, TC), BF16), ("d_g", (128, C, TC), BF16),
                             ("d_iq", (128, H * H), F32), ("d_cw", (E, TC), F32),
                             ("d_v1", (128, C, TT), BF16), ("d_mix", (128, C, TT), BF16)):
            dbg[nm] = nc.dram_tensor(nm, list(shp), dt_, kind="ExternalOutput").ap()

    c_w = nc.dram_tensor("c_w", [N_WSLAB, 128, 2048], BF16).ap()
    cc_in = nc.dram_tensor("cc_in", [128, 512], F32).ap()
    cc_out = nc.dram_tensor("cc_out", [128, 512], F32).ap()

    with contextlib.ExitStack() as es:
        def sbn(name, shape, dtype):
            return es.enter_context(nc.sbuf_tensor(name, list(shape), dtype))

        def sba(name, shape, dtype, off):
            return nc.alloc_sbuf_tensor_at(name, list(shape), dtype, offset=off)

        xn_bf = sbn("xn_bf", (128, C, TC), BF16)
        lnr_r = sbn("lnr_r", (1, 7, 512), F32R)
        lnf = sbn("lnf", (1, 2, 512), F32)
        bcols_s = sbn("bcols_s", (128, 7, C), F32)
        SCt = sbn("SCt", (128, H, E), F32)
        sfc = sbn("sfc", (128, H, E), F32)
        msk = sbn("msk", (128, H, E), F32)
        selw = sbn("selw", (128, H, E), F32)
        cw_tm = sbn("cw_tm", (128, H, E), F32)
        rt = [sbn(f"rt{i}", (128, H), F32) for i in range(8)]
        cw_r = sbn("cw_r", (E, TC), F32R)
        invq_tm = sbn("invq_tm", (128, H * H), F32)
        S_pk = sbn("S_pk", (128, 512), F32)
        S_fl = sbn("S_fl", (128, 512), F32)
        BDS = sbn("BDS", (128, C, 128), BF16)
        gatet_s = sbn("gatet_s", (128, C, E), F32)
        gbias_s = sbn("gbias_s", (128, H, E), F32)
        cones_s = sbn("cones_s", (128, 1), F32)
        cones_r = sbn("cones_r", (128, 1), F32R)
        ones128_r = sbn("ones128_r", (1, 128), F32R)
        ident_s = sbn("ident_s", (128, 128), F32)
        hmask_r = sbn("hmask_r", (H, 2 * D), F32R)

        wslot = [sba(f"wslot{i}", (128, C, D), BF16, RA + i * 16384)
                 for i in range(4)]
        r2s = sba("r2s", (128, E * 4, TT), BF16, RA)
        u_bf = sba("u_bf", (128, SIC, TT), BF16, RA + 32768)
        fr_t = [sba(f"fr_t{i}", (128, TT), F32, RA + 49152 + i * 2048)
                for i in range(2)]
        fs_t = [sba(f"fs_t{i}", (128, TT), F32, RA + 53248 + i * 2048)
                for i in range(2)]
        y_t = sba("y_t", (128, C, TT), F32R, RA)
        v1_bf = sba("v1_bf", (128, C, TT), BF16, RA + 16384)
        mix_bf = sba("mix_bf", (128, C, TT), BF16, RA + 24576)
        wst = [sba(f"wst{i}", (128, 2048), BF16, RA + 32768 + i * 4096)
               for i in range(2)]
        ysq = [sba(f"ysq{i}", (128, TT), F32R, RA + 40960 + i * 2048)
               for i in range(2)]
        s_gs = sba("s_gs", (128, TT), F32, RA + 45056)
        s_t1 = sba("s_t1", (128, TT), F32, RA + 47104)
        v_pp = [sba(f"v_pp{i}", (128, TT), BF16, RA + 49152 + i * 1024)
                for i in range(2)]
        invq_r = sba("invq_r", (H, TC), F32R, RA + 57344)
        iqx = [sba(f"iqx{i}", (128, TT), F32, RA + 51200 + i * 2048)
               for i in range(2)]

        xtile = [sba(f"xtile{i}", (128, C, TT), F32, RB + i * 16384)
                 for i in range(2)]
        q1_bf = sba("q1_bf", (128, C, TT), BF16, RB)
        k1_bf = sba("k1_bf", (128, C, TT), BF16, RB + 8192)
        q_tm = sba("q_tm", (128, NG, D), BF16, RB + 16384)
        k_tm = sba("k_tm", (128, NG, D), BF16, RB + 24576)
        fstr = [sba(f"fstr{i}", (128, 2048), BF16, RB + i * 4096)
                for i in range(8)]

        wstg = [sba(f"wstg{i}", (128, 2048), F32, RC + i * 8192)
                for i in range(2)]
        wcvt = [sba(f"wcvt{i}", (128, 2048), BF16, RC + 16384 + i * 4096)
                for i in range(2)]
        xsq = [sba(f"xsq{i}", (128, TT), F32R, RC + 24576 + i * 2048)
               for i in range(2)]
        xnf = [sba(f"xnf{i}", (128, TT), F32, RC + 28672 + i * 2048)
               for i in range(2)]
        cw_fm = sba("cw_fm", (E, TC), F32, RC + 24576)
        invq_fm = sba("invq_fm", (H, TC), F32, RC + 24576)
        g_bf = sba("g_bf", (128, C, TC), BF16, RC)
        brows_s = sba("brows_s", (1, 7, 512), F32, RC)
        hmask_s = sba("hmask_s", (H, 2 * D), F32, RC + 16384)
        elu_m = [sba(f"elu_m{i}", (128, 512), F32, RC + 24576 + i * 2048)
                 for i in range(2)]
        elu_rt = [sba(f"elu_rt{i}", (128, 512), F32, RC + 28672 + i * 2048)
                  for i in range(2)]
        qkf = sbn("qkf", (128, 512), BF16)

        pb = [es.enter_context(nc.psum_tensor(f"pb{i}", [128, 512], F32))
              for i in range(8)]

        s_in = es.enter_context(nc.semaphore("s_in"))
        s_x = [es.enter_context(nc.semaphore(f"s_x{i}")) for i in range(2)]
        s_wp = [es.enter_context(nc.semaphore(f"s_wp{i}")) for i in range(2)]
        s_cw = [es.enter_context(nc.semaphore(f"s_cw{i}")) for i in range(2)]
        s_fs = [es.enter_context(nc.semaphore(f"s_fs{i}")) for i in range(8)]
        s_ws = [es.enter_context(nc.semaphore(f"s_ws{i}")) for i in range(2)]
        s_ob = [es.enter_context(nc.semaphore(f"s_ob{i}")) for i in range(2)]
        s_scc = es.enter_context(nc.semaphore("s_scc"))
        s_gd = es.enter_context(nc.semaphore("s_gd"))
        s_gd2 = es.enter_context(nc.semaphore("s_gd2"))
        s_tn = es.enter_context(nc.semaphore("s_tn"))
        s_vc = es.enter_context(nc.semaphore("s_vc"))
        s_sc = es.enter_context(nc.semaphore("s_sc"))
        s_cc = es.enter_context(nc.semaphore("s_cc"))

        assert nc.sbuf_bytes_remaining >= SB_TOP - RC, (
            "P0 allocations overflow into overlay regions: remaining="
            f"{nc.sbuf_bytes_remaining}, need >= {SB_TOP - RC}")

        EM = _Em()
        bank_guard = [None] * 8

        def dma(q, sem, dst, src, waits=()):
            prev = EM.cnt.get(sem.name, 0)
            w = list(waits) + ([(sem, prev)] if prev else [])
            return EM.emit(q, lambda e, d=dst, s=src: e.dma_start(out=d, in_=s),
                           waits=w, inc=(sem, 16))

        def mm(out, lhsT, rhs, start, stop, waits=(), inc=None, bank=None):
            w = list(waits)
            if start and bank is not None and bank_guard[bank] is not None:
                w.append(bank_guard[bank])
            return EM.emit(
                "tensor",
                lambda e, o=out, l=lhsT, r=rhs, st=start, sp=stop:
                    e.matmul(o, l, r, start=st, stop=sp),
                waits=w, inc=inc)

        def read_bank(bank, tok):
            bank_guard[bank] = tok

        def vchain(eng, sem, fns, waits=()):
            tok = None
            for i, fn in enumerate(fns):
                w = list(waits) if i == 0 else [tok]
                tok = EM.emit(eng, fn, waits=w, inc=sem)
            return tok

        t_ident = dma("sync", s_in, ident_s[:], di["ident"])
        t_avl = dma("sync", s_in, brows_s[:], di["brows"])
        t_hml = dma("sync", s_in, hmask_s[:], di["hmask"])
        t_gatet = dma("sync", s_in, gatet_s[:], di["gatet"])
        t_gbias = dma("sync", s_in, gbias_s[:], di["gbias"])
        t_bcl = dma("sync", s_in, bcols_s[:], di["bcols"])
        t_avr = EM.emit("scalar", lambda e: e.copy(lnr_r[0:1, 0:7, :], brows_s[:]),
                        waits=[t_avl], inc=s_sc)
        t_hmr = EM.emit("scalar", lambda e: e.copy(hmask_r[:], hmask_s[:]),
                        waits=[t_hml], inc=s_sc)

        def init_consts(e):
            e.memset(cones_s[:], 1.0)
            return e.memset(lnf[0:1, 0, :], 1.0)
        t_onesf = EM.emit("vector", init_consts, waits=[], inc=s_vc)

        def init_consts2(e):
            e.copy(ones128_r[:], lnf[0:1, 0, 0:128])
            e.copy(cones_r[:], cones_s[:])
            return e.copy(lnr_r[0:1, 2, :], lnf[0:1, 0, :])
        t_ones = EM.emit("scalar", init_consts2, waits=[t_onesf, t_avr], inc=s_sc)
        t_cones = t_ones
        t_o128 = t_ones

        stg_free = [None, None]
        cvt_free = [None, None]
        piece_i = [0]
        cache_tok = {}

        def w_piece(src_ap, dst_sbuf=None, cache_slab=None):
            i = piece_i[0] % 2
            ceng = (("scalar", s_sc), ("vector", s_vc))[piece_i[0] % 2]
            piece_i[0] += 1
            w0 = [t_avr, t_hmr] if piece_i[0] <= 2 else []
            t_l = dma("sync", s_wp[i], wstg[i][:], src_ap,
                      waits=[stg_free[i]] + w0)
            use_tc = ceng[0] == "vector"
            if dst_sbuf is not None:
                t_c = EM.emit(ceng[0],
                              lambda e, i=i, d=dst_sbuf, u=use_tc: e.tensor_copy(d, wstg[i][:])
                              if u else e.copy(d, wstg[i][:]),
                              waits=[t_l], inc=ceng[1])
                stg_free[i] = t_c
                return t_c
            wcv = [t_hmr] if piece_i[0] <= 18 else []
            t_c = EM.emit(ceng[0],
                          lambda e, i=i, u=use_tc: e.tensor_copy(wcvt[i][:], wstg[i][:])
                          if u else e.copy(wcvt[i][:], wstg[i][:]),
                          waits=[t_l, cvt_free[i]] + wcv, inc=ceng[1])
            stg_free[i] = t_c
            t_s = dma("scalar", s_cw[i], c_w[cache_slab], wcvt[i][:], waits=[t_c])
            cvt_free[i] = t_s
            cache_tok[cache_slab] = t_s
            return t_s

        w1_tok = {}
        PIECES = []
        for wi in range(4):
            for q in range(4):
                dst = wslot[wi][:, 2 * q:2 * q + 2, :].rearrange("p a b -> p (a b)")
                PIECES.append(("w1", wi, di["w1"][4 * wi + q], dst, None))
        for s in range(N_WSLAB):
            PIECES.append(("wffn", None, di["wffn"][s], None, s))
        piece_cursor = [0]

        def emit_pieces(n):
            while n > 0 and piece_cursor[0] < len(PIECES):
                kind, wi, src_ap, dst, slab = PIECES[piece_cursor[0]]
                piece_cursor[0] += 1
                tok = w_piece(src_ap, dst_sbuf=dst, cache_slab=slab)
                if kind == "w1":
                    w1_tok[wi] = tok
                n -= 1


        x_free = [(), ()]
        xn_done = []
        t_sig = None
        xsq_free = [None, None]
        xnf_free = [None, None]
        xnf_sc_free = [None, None]
        last_lnr_rd = None

        x_tok = [None] * NT
        x_tok[0] = dma("sync", s_x[0], xtile[0][:], di["x"][:, :, 0:TT])
        stats_out = [None] * NT

        def emit_stats(t):
            par = t % 2
            bsum, bsq = (0, 5) if t % 2 == 0 else (6, 7)
            t_x = x_tok[t]
            t_sum = None
            for c in range(C):
                t_sum = mm(pb[bsum][0:1, :], cones_s[:], xtile[par][:, c, :],
                           start=(c == 0), stop=(c == C - 1),
                           waits=[t_x, t_cones] if c == 0 else (),
                           bank=bsum if c == 0 else None,
                           inc=(s_tn if c == C - 1 else None))
            tok_ss = None
            last_tsq = None
            for c in range(C):
                pp = c % 2
                last_tsq = EM.emit(
                    "scalar",
                    lambda e, c=c, pp=pp, par=par: e.activation(
                        xsq[pp][:], xtile[par][:, c, :], Act.Square),
                    waits=[t_x, xsq_free[pp]], inc=s_sc)
                tok_ss = mm(pb[bsq][0:1, :], cones_r[:], xsq[pp][:],
                            start=(c == 0), stop=(c == C - 1), waits=[last_tsq],
                            bank=bsq if c == 0 else None, inc=s_tn)
                xsq_free[pp] = tok_ss
            stats_out[t] = (t_sum, tok_ss, last_tsq, (bsum, bsq))

        emit_stats(0)
        for t in range(NT):
            par = t % 2
            tsl = slice(t * TT, (t + 1) * TT)
            if t + 1 < NT:
                x_tok[t + 1] = dma("sync", s_x[(t + 1) % 2], xtile[(t + 1) % 2][:],
                                   di["x"][:, :, (t + 1) * TT:(t + 2) * TT],
                                   waits=list(x_free[(t + 1) % 2]))
                emit_stats(t + 1)
            t_x = x_tok[t]
            t_sum, tok_ss, last_tsq, (bsum, bsq) = stats_out[t]

            t_row = vchain("vector", s_vc, [
                lambda e, b=bsum: e.tensor_scalar(lnf[0:1, 0, :], pb[b][0:1, :], 1.0 / D, None, Alu.mult),
                lambda e: e.tensor_tensor(lnf[0:1, 1, :], lnf[0:1, 0, :], lnf[0:1, 0, :], Alu.mult),
                lambda e, b=bsq: e.scalar_tensor_tensor(lnf[0:1, 1, :], pb[b][0:1, :], 1.0 / D,
                                                        lnf[0:1, 1, :], Alu.mult, Alu.subtract),
                lambda e: e.tensor_scalar(lnf[0:1, 1, :], lnf[0:1, 1, :], EPS_LN1, None, Alu.add),
                lambda e: e.reciprocal(lnf[0:1, 1, :], lnf[0:1, 1, :]),
            ], waits=[tok_ss, last_lnr_rd])
            read_bank(bsum, t_row)
            read_bank(bsq, t_row)
            t_rstd = EM.emit(
                "scalar", lambda e: e.activation(lnr_r[0:1, 0, :], lnf[0:1, 1, :], Act.Sqrt),
                waits=[t_row], inc=s_sc)
            t_lncp = EM.emit(
                "vector",
                lambda e: e.scalar_tensor_tensor(
                    lnr_r[0:1, 1, :], lnf[0:1, 0, :], -1.0, lnr_r[0:1, 0, :],
                    Alu.mult, Alu.mult),
                waits=[t_rstd], inc=s_vc)
            t_A1 = mm(pb[1][:], ones128_r[:], lnr_r[0:1, 0, :], True, True,
                      waits=[t_lncp, t_o128], bank=1, inc=s_tn)
            t_B1 = mm(pb[2][:], ones128_r[:], lnr_r[0:1, 1, :], True, True,
                      bank=2, inc=s_tn)
            last_lnr_rd = t_B1
            last_gmm = None
            t_xnbf = None
            for c in range(C):
                pp = c % 2
                def xn_tt(e, c=c, pp=pp, par=par):
                    e.tensor_tensor(xnf[pp][:], xtile[par][:, c, :], pb[1][:], Alu.mult)
                    return e.tensor_tensor(xnf[pp][:], xnf[pp][:], pb[2][:], Alu.add)
                t_xnf = EM.emit("vector", xn_tt,
                                waits=[t_A1, t_B1, xnf_free[pp], xnf_sc_free[pp]],
                                inc=s_vc)
                t_xnbf = EM.emit(
                    "scalar",
                    lambda e, c=c, pp=pp, t=t: e.copy(
                        xn_bf[:, c, t * TT:(t + 1) * TT], xnf[pp][:]),
                    waits=[t_xnf], inc=s_sc)
                for g in range(NG):
                    last_gmm = mm(
                        pb[3][:, g * 8:(g + 1) * 8],
                        xnf[pp][:, g * 128:(g + 1) * 128],
                        gatet_s[:, c, :],
                        start=(c == 0 and g == 0),
                        stop=(c == C - 1 and g == NG - 1),
                        waits=[t_xnf, t_gatet] if g == 0 else (),
                        bank=3 if (c == 0 and g == 0) else None,
                        inc=(s_tn if g == NG - 1 else None))
                xnf_free[pp] = last_gmm
                xnf_sc_free[pp] = t_xnbf
            xn_done.append(t_xnbf)
            x_free[par] = (t_sum, last_tsq, t_xnf)
            t_sig = EM.emit(
                "scalar",
                lambda e, t=t: e.activation(
                    SCt[:, t * NG:(t + 1) * NG, :].rearrange("p a b -> p (a b)"),
                    pb[3][:, 0:32], Act.Sigmoid),
                waits=[last_gmm], inc=s_sc)
            read_bank(3, t_sig)
            emit_pieces(4)
            read_bank(1, t_xnf)
            read_bank(2, t_xnf)

        rf = [lambda e: e.tensor_tensor(sfc[:], SCt[:], gbias_s[:], Alu.add)]
        for gi in range(2):
            base = gi * 4
            first = True
            for a in range(4):
                for b in range(a + 1, 4):
                    rf.append(lambda e, base=base, a=a, b=b: e.tensor_tensor(
                        rt[2][:], sfc[:, :, base + a], sfc[:, :, base + b], Alu.add))
                    if first:
                        rf.append(lambda e, gi=gi: e.tensor_copy(rt[gi][:], rt[2][:]))
                        first = False
                    else:
                        rf.append(lambda e, gi=gi: e.tensor_tensor(
                            rt[gi][:], rt[gi][:], rt[2][:], Alu.max))
        rf.append(lambda e: e.tensor_tensor(rt[3][:], rt[0][:], rt[1][:], Alu.is_ge))
        rf.append(lambda e: e.tensor_scalar(rt[4][:], rt[3][:], -1.0, 1.0, Alu.mult, Alu.add))
        for x in range(E):
            gm = rt[3] if x < 4 else rt[4]
            rf.append(lambda e, x=x, gm=gm: e.tensor_tensor(
                msk[:, :, x], sfc[:, :, x], gm[:], Alu.mult))
        rf.append(lambda e: e.tensor_tensor(rt[5][:], msk[:, :, 0], msk[:, :, 1], Alu.max))
        for x in range(2, E):
            rf.append(lambda e, x=x: e.tensor_tensor(rt[5][:], rt[5][:], msk[:, :, x], Alu.max))
        for x in range(E):
            rf.append(lambda e, x=x: e.tensor_tensor(
                selw[:, :, x], msk[:, :, x], rt[5][:], Alu.is_ge))
        for x in range(E):
            rf.append(lambda e, x=x: e.scalar_tensor_tensor(
                cw_tm[:, :, x], selw[:, :, x], -1e30, msk[:, :, x], Alu.mult, Alu.add))
        rf.append(lambda e: e.tensor_tensor(rt[6][:], cw_tm[:, :, 0], cw_tm[:, :, 1], Alu.max))
        for x in range(2, E):
            rf.append(lambda e, x=x: e.tensor_tensor(rt[6][:], rt[6][:], cw_tm[:, :, x], Alu.max))
        for x in range(E):
            rf.append(lambda e, x=x: e.tensor_tensor(
                msk[:, :, x], cw_tm[:, :, x], rt[6][:], Alu.is_ge))
            rf.append(lambda e, x=x: e.tensor_tensor(
                selw[:, :, x], selw[:, :, x], msk[:, :, x], Alu.add))
        for x in range(E):
            rf.append(lambda e, x=x: e.tensor_tensor(
                selw[:, :, x], selw[:, :, x], SCt[:, :, x], Alu.mult))
        rf.append(lambda e: e.tensor_tensor(rt[7][:], selw[:, :, 0], selw[:, :, 1], Alu.add))
        for x in range(2, E):
            rf.append(lambda e, x=x: e.tensor_tensor(rt[7][:], rt[7][:], selw[:, :, x], Alu.add))
        rf.append(lambda e: e.tensor_scalar(rt[7][:], rt[7][:], 1e-20, None, Alu.add))
        rf.append(lambda e: e.reciprocal(rt[7][:], rt[7][:]))
        rf.append(lambda e: e.tensor_scalar(rt[7][:], rt[7][:], RSCALE, None, Alu.mult))
        for x in range(E):
            rf.append(lambda e, x=x: e.tensor_tensor(
                cw_tm[:, :, x], selw[:, :, x], rt[7][:], Alu.mult))
        t_rtr = vchain("vector", s_vc, rf, waits=[t_sig, t_gbias])


        s1_ev = {}
        s2_guard = {}
        eb_ctr = [0]
        elu_free = [None, None]
        qk_done = {}
        S_tile_tok = None
        qk_free = None
        t_S_last = None
        t_qk_read = None
        last_s2_mm = None
        t_red_last = None

        ho_B = list(x_free[0]) + list(x_free[1])
        first_B_ev = True
        first_B_tm = True
        first_B_elu = True
        for t in range(NT):
            tsl = slice(t * TT, (t + 1) * TT)
            for proj, wi1, dst1 in (("q", 0, q1_bf), ("k", 1, k1_bf)):
                for mo in range(C):
                    bk = mo % 4
                    wl = [w1_tok[wi1], xn_done[t], t_avr, t_ones]
                    for kc in range(C):
                        mm(pb[bk][:], wslot[wi1][:, kc, mo * 128:(mo + 1) * 128],
                           xn_bf[:, kc, tsl], start=(kc == 0), stop=(kc == C - 1),
                           waits=wl if kc == 0 else (), bank=bk if kc == 0 else None,
                           inc=(s_tn if kc == C - 1 else None))
                    bcol = 0 if proj == "q" else 1
                    tok_b = (s_tn, EM.cnt["s_tn"])
                    wl_ev = [tok_b, t_bcl] + (ho_B if first_B_ev else [])
                    first_B_ev = False
                    if mo % 2 == 0:
                        t_ev = EM.emit(
                            "scalar",
                            lambda e, d=dst1, mo=mo, bk=bk, bcol=bcol: e.activation(
                                d[:, mo, :], pb[bk][:], Act.Identity,
                                bias=bcols_s[:, bcol, mo:mo + 1], scale=1.0),
                            waits=wl_ev, inc=s_sc)
                    else:
                        t_ev = EM.emit(
                            "vector",
                            lambda e, d=dst1, mo=mo, bk=bk, bcol=bcol: e.tensor_scalar(
                                d[:, mo, :], pb[bk][:],
                                bcols_s[:, bcol, mo:mo + 1], None, Alu.add),
                            waits=wl_ev, inc=s_vc)
                    read_bank(bk, t_ev)
                    s1_ev[(proj, t)] = t_ev
                    emit_pieces(1)
            for g in range(NG):
                gi = t * NG + g
                tok_q_done = tok_k_done = None
                for proj, wi2, src1, dst_tm in (
                        ("q", 2, q1_bf, q_tm), ("k", 3, k1_bf, k_tm)):
                    brow = 3 if proj == "q" else 5
                    bks = ((2, 3), (4, 5), (0, 1))[(2 * g + (0 if proj == "q" else 1)) % 3]
                    for half in range(2):
                        bk = bks[half]
                        wl = [w1_tok[wi2], s1_ev[(proj, t)]]
                        for kc in range(C):
                            mm(pb[bk][:],
                               src1[:, kc, g * 128:(g + 1) * 128],
                               wslot[wi2][:, kc, half * 512:(half + 1) * 512],
                               start=(kc == 0), stop=False,
                               waits=wl if kc == 0 else (),
                               bank=bk if kc == 0 else None)
                        tok_b = mm(pb[bk][:], ones128_r[:],
                                   lnr_r[0:1, brow + half, :],
                                   start=False, stop=True, inc=s_tn)
                        last_s2_mm = tok_b
                        eb = eb_ctr[0] % 2
                        eb_ctr[0] += 1
                        wl_min = [tok_b, elu_free[eb]]
                        if first_B_elu:
                            wl_min += [xsq_free[0], xsq_free[1], xnf_free[0],
                                       xnf_free[1], xnf_sc_free[0], xnf_sc_free[1]]
                        t_min = EM.emit(
                            "vector",
                            lambda e, bk=bk, eb=eb: e.tensor_scalar_min(
                                elu_m[eb][:], pb[bk][:], 0.0),
                            waits=wl_min, inc=s_vc)
                        first_B_elu = False
                        t_rel = EM.emit(
                            "scalar",
                            lambda e, bk=bk, eb=eb: e.activation(
                                elu_rt[eb][:], pb[bk][:], Act.Relu),
                            waits=[t_min], inc=s_sc)
                        read_bank(bk, t_rel)
                        t_exp = EM.emit(
                            "scalar",
                            lambda e, eb=eb: e.activation(elu_m[eb][:], elu_m[eb][:], Act.Exp),
                            waits=[t_rel], inc=s_sc)
                        wl_tm = [t_exp, S_tile_tok] + (ho_B if first_B_tm else [])
                        first_B_tm = False
                        t_add = EM.emit(
                            "vector",
                            lambda e, d=dst_tm, g=g, half=half, eb=eb: e.tensor_tensor(
                                d[:, g, half * 512:(half + 1) * 512],
                                elu_rt[eb][:], elu_m[eb][:], Alu.add),
                            waits=wl_tm, inc=s_vc)
                        elu_free[eb] = t_add
                    if proj == "q":
                        tok_q_done = t_add
                    else:
                        tok_k_done = t_add
                for half in range(2):
                    t_qk = EM.emit(
                        "vector",
                        lambda e, g=g, half=half: e.tensor_tensor(
                            qkf[:], q_tm[:, g, half * 512:(half + 1) * 512],
                            k_tm[:, g, half * 512:(half + 1) * 512], Alu.mult),
                        waits=[tok_q_done, tok_k_done, qk_free], inc=s_vc)
                    t_red = EM.emit(
                        "vector",
                        lambda e, gi=gi, half=half: e.tensor_reduce(
                            invq_tm[:, gi * H + half * 8:gi * H + (half + 1) * 8],
                            qkf[:].rearrange("p (h d) -> p h d", d=DH),
                            AxX, Alu.add),
                        waits=[t_qk], inc=s_vc)
                    qk_free = t_red
                    t_red_last = t_red
                qk_done[g] = (tok_q_done, tok_k_done)
            for g in range(NG):
                gi = t * NG + g
                for cp in range(C):
                    bk = 6 + cp // 4
                    t_S_last = mm(
                        pb[bk][:, (cp % 4) * 128:(cp % 4 + 1) * 128],
                        k_tm[:, g, cp * 128:(cp + 1) * 128],
                        q_tm[:, g, cp * 128:(cp + 1) * 128],
                        start=(gi == 0 and cp % 4 == 0),
                        stop=(gi == H - 1 and cp % 4 == 3),
                        waits=list(qk_done[g]) if cp == 0 else (),
                        bank=bk if (gi == 0 and cp % 4 == 0) else None,
                        inc=(s_tn if cp == C - 1 else None))
            S_tile_tok = t_S_last
        emit_pieces(len(PIECES))

        t_cwv = None
        for g in range(H):
            bk = 4 + (g % 2)
            t_tr = EM.emit(
                "tensor",
                lambda e, g=g, bk=bk: e.transpose(
                    pb[bk][0:8, 0:128], cw_tm[:, g, :], ident_s[:]),
                waits=[t_rtr, t_ident, bank_guard[bk]], inc=s_tn)
            t_cwv = EM.emit(
                "vector",
                lambda e, g=g, bk=bk: e.tensor_copy(
                    cw_fm[:, g * 128:(g + 1) * 128], pb[bk][0:8, 0:128]),
                waits=[t_tr] + ([elu_free[0], elu_free[1], t_red_last] if g == 0 else []),
                inc=s_vc)
            read_bank(bk, t_cwv)
        t_cwr = EM.emit("scalar", lambda e: e.copy(cw_r[:], cw_fm[:]),
                        waits=[t_cwv], inc=s_sc)

        t_invq = vchain("vector", s_vc, [
            lambda e: e.tensor_scalar(invq_tm[:], invq_tm[:], EPS_QK, None, Alu.add),
            lambda e: e.reciprocal(invq_tm[:], invq_tm[:]),
        ], waits=[t_red_last])
        t_iqv = None
        for gi in range(H):
            bk = 4 + gi % 2
            t_tr = EM.emit(
                "tensor",
                lambda e, gi=gi, bk=bk: e.transpose(
                    pb[bk][0:H, 0:128], invq_tm[:, gi * H:(gi + 1) * H], ident_s[:]),
                waits=[t_invq, t_ident, bank_guard[bk]], inc=s_tn)
            t_iqv = EM.emit(
                "vector",
                lambda e, gi=gi, bk=bk: e.tensor_copy(
                    invq_fm[:, gi * 128:(gi + 1) * 128], pb[bk][0:H, 0:128]),
                waits=[t_tr] + ([t_cwr] if gi == 0 else []), inc=s_vc)
            read_bank(bk, t_iqv)
        t_iqr = EM.emit("scalar", lambda e: e.copy(invq_r[:], invq_fm[:]),
                        waits=[t_iqv, last_s2_mm], inc=s_sc)

        def s_extract(e):
            r = None
            for cp in range(C):
                bk = 6 + cp // 4
                cb = (cp % 4) * 128
                e.tensor_copy(S_pk[0:64, cp * 64:(cp + 1) * 64],
                              pb[bk][0:64, cb:cb + 64])
                r = e.tensor_copy(S_pk[64:128, cp * 64:(cp + 1) * 64],
                                  pb[bk][64:128, cb + 64:cb + 128])
            return r
        t_sx = EM.emit("vector", s_extract, waits=[t_S_last], inc=s_vc)
        read_bank(6, t_sx)
        read_bank(7, t_sx)
        t_sst = dma("scalar", s_scc, cc_in, S_pk[:], waits=[t_sx])
        t_cc = EM.emit(
            "gpsimd",
            lambda e: e.collective_compute(
                "AllReduce", Alu.add,
                replica_groups=[list(g) for g in cc_groups],
                ins=[cc_in], outs=[cc_out]),
            waits=[t_sst], inc=s_cc)
        t_sfl = dma("gpsimd", s_gd, S_fl[:], cc_out, waits=[t_cc])

        handover_D = [last_s2_mm, t_S_last, t_red_last, t_iqr, elu_free[0],
                      elu_free[1], cvt_free[0], cvt_free[1], stg_free[0],
                      stg_free[1]]
        fstr_free = [None] * 8
        frfs_free = [None, None]
        cwb_free = [None, None]
        r2s_free = None
        g_written = []
        last_ffn_mm = None
        last_ffn_v = None

        for t in range(NT):
            tsl = slice(t * TT, (t + 1) * TT)
            r2s_tok = None
            u_tok = None
            for e_i in range(E):
                cb = 4 + e_i % 2
                t_cwb = mm(pb[cb][:],
                           hmask_r[0:8, D + e_i * 128:D + (e_i + 1) * 128],
                           cw_r[:, tsl], True, True,
                           waits=[t_cwr, t_hmr, cwb_free[e_i % 2], bank_guard[cb]],
                           inc=s_tn)
                for hf in range(2):
                    sbuf_i = (e_i * 2 + hf) % 2
                    slab = N_UP0 + e_i * 2 + hf
                    wl = [cache_tok[slab], fstr_free[sbuf_i]]
                    if t == 0 and e_i == 0 and hf == 0:
                        wl += handover_D
                    t_w = dma("sync", s_fs[sbuf_i], fstr[sbuf_i][:], c_w[slab], waits=wl)
                    for ic2 in range(2):
                        icg = hf * 2 + ic2
                        bk = icg % 4
                        for kc in range(C):
                            mm(pb[bk][:],
                               fstr[sbuf_i][:, kc * 256 + ic2 * 128:
                                            kc * 256 + (ic2 + 1) * 128],
                               xn_bf[:, kc, tsl],
                               start=(kc == 0), stop=(kc == C - 1),
                               waits=[t_w, xn_done[t]] if kc == 0 else (),
                               bank=bk if kc == 0 else None,
                               inc=(s_tn if kc == C - 1 else None))
                        tok_up = (s_tn, EM.cnt["s_tn"])
                        pp = icg % 2
                        t_fr = EM.emit(
                            "scalar",
                            lambda e, bk=bk, pp=pp: e.activation(
                                fr_t[pp][:], pb[bk][:], Act.Relu),
                            waits=[tok_up, frfs_free[pp],
                                   (handover_D[0] if (t == 0 and e_i == 0) else None)],
                            inc=s_sc)
                        read_bank(bk, t_fr)
                        t_fs = EM.emit(
                            "scalar",
                            lambda e, pp=pp: e.activation(
                                fs_t[pp][:], fr_t[pp][:], Act.Square),
                            waits=[t_fr], inc=s_sc)
                        slot = e_i * 4 + icg
                        wl2 = [t_fs, t_cwb]
                        if t == 0 and slot == 0:
                            wl2 += handover_D
                        if r2s_free is not None and slot == 0:
                            wl2.append(r2s_free)
                        r2s_tok = EM.emit(
                            "vector",
                            lambda e, slot=slot, cb=cb, pp=pp: e.tensor_tensor(
                                r2s[:, slot, :], fs_t[pp][:], pb[cb][:], Alu.mult),
                            waits=wl2, inc=s_vc)
                        frfs_free[pp] = r2s_tok
                        cwb_free[e_i % 2] = r2s_tok
                        read_bank(cb, r2s_tok)
                    fstr_free[sbuf_i] = tok_up
            for j in range(SIC):
                sbuf_i = 2 + (j // 2) % 2
                if j % 2 == 0:
                    slab = N_SU0 + j // 2
                    wl = [cache_tok[slab], fstr_free[sbuf_i]]
                    if t == 0 and j == 0:
                        wl += handover_D
                    t_su = dma("sync", s_fs[sbuf_i], fstr[sbuf_i][:], c_w[slab], waits=wl)
                bk = j % 4
                for kc in range(C):
                    mm(pb[bk][:],
                       fstr[sbuf_i][:, (j % 2) * 1024 + kc * 128:
                                    (j % 2) * 1024 + (kc + 1) * 128],
                       xn_bf[:, kc, tsl],
                       start=(kc == 0), stop=(kc == C - 1),
                       waits=[t_su, xn_done[t]] if kc == 0 else (),
                       bank=bk if kc == 0 else None,
                       inc=(s_tn if kc == C - 1 else None))
                tok_su = (s_tn, EM.cnt["s_tn"])
                pp = j % 2
                t_fr = EM.emit(
                    "scalar",
                    lambda e, bk=bk, pp=pp: e.activation(
                        fr_t[pp][:], pb[bk][:], Act.Relu),
                    waits=[tok_su, frfs_free[pp]], inc=s_sc)
                read_bank(bk, t_fr)
                wl2 = [t_fr]
                if r2s_free is not None and t > 0 and j == 0:
                    wl2.append(r2s_free)
                u_tok = EM.emit(
                    "vector",
                    lambda e, j=j, pp=pp: e.tensor_tensor(
                        u_bf[:, j, :], fr_t[pp][:], fr_t[pp][:], Alu.mult),
                    waits=wl2, inc=s_vc)
                frfs_free[pp] = u_tok
                if j % 2 == 1:
                    fstr_free[sbuf_i] = tok_su
            t_sd_prev = {}
            for j in range(C):
                sd_i = 4 + j % 2
                slab_sd = N_SD0 + j
                wl = [cache_tok[slab_sd], fstr_free[sd_i]]
                t_sd = dma("sync", s_fs[sd_i], fstr[sd_i][:], c_w[slab_sd], waits=wl)
                bk = 4 + j % 2
                first = True
                for sic in range(SIC):
                    mm(pb[bk][:],
                       fstr[sd_i][:, sic * 128:(sic + 1) * 128],
                       u_bf[:, sic, :],
                       start=first, stop=False,
                       waits=[t_sd, u_tok] if first else (),
                       bank=bk if first else None)
                    first = False
                for eh in range(2):
                    dn_i = 6 + eh
                    slab_dn = N_DN0 + j * 2 + eh
                    wl = [cache_tok[slab_dn], fstr_free[dn_i]]
                    t_dn = dma("sync", s_fs[dn_i], fstr[dn_i][:], c_w[slab_dn], waits=wl)
                    for e4 in range(4):
                        for ic in range(4):
                            is_last = (eh == 1 and e4 == 3 and ic == 3)
                            is_last_eh = (e4 == 3 and ic == 3)
                            last_ffn_mm = mm(
                                pb[bk][:],
                                fstr[dn_i][:, e4 * 512 + ic * 128:
                                           e4 * 512 + (ic + 1) * 128],
                                r2s[:, (eh * 4 + e4) * 4 + ic, :],
                                start=False, stop=is_last,
                                waits=[t_dn, r2s_tok] if (e4 == 0 and ic == 0) else (),
                                inc=(s_tn if is_last_eh else None))
                    fstr_free[dn_i] = last_ffn_mm
                fstr_free[sd_i] = last_ffn_mm
                wl_g = [last_ffn_mm]
                if t == 0 and j == 0:
                    wl_g += [cvt_free[0], cvt_free[1], t_iqr]
                t_gw = EM.emit(
                    "scalar",
                    lambda e, j=j, bk=bk, tsl=tsl: e.copy(
                        g_bf[:, j, tsl], pb[bk][:]),
                    waits=wl_g, inc=s_sc)
                read_bank(bk, t_gw)
                g_written.append(t_gw)
            r2s_free = last_ffn_mm

        bds_fns = [lambda e: e.memset(BDS[:], 0.0)]
        for cp in range(C):
            bds_fns.append(lambda e, cp=cp: e.tensor_copy(
                BDS[0:64, cp, 0:64], S_fl[0:64, cp * 64:(cp + 1) * 64]))
            bds_fns.append(lambda e, cp=cp: e.tensor_copy(
                BDS[64:128, cp, 64:128], S_fl[64:128, cp * 64:(cp + 1) * 64]))
        t_bds = vchain("vector", s_vc, bds_fns, waits=[t_sfl])

        wst_free = [None, None]
        ysq_free = [None, None]
        xE_free = [None, None]
        v1_guard = r2s_free
        vpp_free = [None, None]
        t_out_last = None

        def emit_combine(ctx, step):
            if step == 0:
                ctx["t_A2"] = mm(pb[2][:], ones128_r[:], lnr_r[0:1, 0, :],
                                 True, True,
                                 waits=[ctx["t_lncp2"], bank_guard[2]], inc=s_tn)
                ctx["t_B2"] = mm(pb[3][:], ones128_r[:], lnr_r[0:1, 1, :],
                                 True, True, waits=[bank_guard[3]], inc=s_tn)
                ctx["last_lnr"] = ctx["t_B2"]
            mo = step
            t_sil = EM.emit(
                "scalar",
                lambda e, mo=mo, tsl=ctx["tsl"]: e.activation(
                    s_gs[:], g_bf[:, mo, tsl], Act.Silu),
                waits=[g_written[-1], ctx.get("t_cmb")], inc=s_sc)

            def s_chain(e, mo=mo, par=ctx["par"]):
                e.tensor_tensor(s_t1[:], y_t[:, mo, :], pb[2][:], Alu.mult)
                e.tensor_tensor(s_t1[:], s_t1[:], pb[3][:], Alu.add)
                e.tensor_scalar(s_t1[:], s_t1[:], bcols_s[:, 5, mo:mo + 1],
                                bcols_s[:, 6, mo:mo + 1], Alu.mult, Alu.add)
                e.tensor_tensor(s_t1[:], s_t1[:], s_gs[:], Alu.mult)
                return e.tensor_tensor(
                    xtile[par][:, mo, :], xtile[par][:, mo, :], s_t1[:], Alu.add)
            ctx["t_cmb"] = EM.emit(
                "vector", s_chain,
                waits=[ctx["t_A2"], ctx["t_B2"], t_sil, ctx["t_x2"], t_bcl],
                inc=s_vc)
            if step == C - 1:
                read_bank(2, ctx["t_cmb"])
                read_bank(3, ctx["t_cmb"])
                t_out = dma("scalar", s_ob[ctx["par"]],
                            out_d[:, :, ctx["tsl"]], xtile[ctx["par"]][:],
                            waits=[ctx["t_cmb"]])
                xE_free[ctx["par"]] = t_out
                ctx["t_out"] = t_out

        pend = None
        last_ctx = None
        for t in range(NT):
            par = t % 2
            tsl = slice(t * TT, (t + 1) * TT)
            t_v1 = None
            for mo in range(C):
                wi = (mo // 2) % 2
                if mo % 2 == 0:
                    slab = N_WV0 + mo // 2
                    wl = [cache_tok[slab], wst_free[wi]]
                    if t == 0 and mo == 0:
                        wl += [v1_guard, frfs_free[0], frfs_free[1]]
                    t_wv = dma("sync", s_ws[wi], wst[wi][:], c_w[slab], waits=wl)
                    if mo == 2:
                        wlx = [xE_free[par]]
                        if t < 2:
                            wlx += [last_ffn_mm]
                        t_x2 = dma("sync", s_x[par], xtile[par][:],
                                   di["x"][:, :, tsl], waits=wlx)
                bk = (0, 1, 4, 5)[mo % 4]
                for kc in range(C):
                    mm(pb[bk][:],
                       wst[wi][:, kc * 256 + (mo % 2) * 128:
                               kc * 256 + (mo % 2 + 1) * 128],
                       xn_bf[:, kc, tsl],
                       start=(kc == 0), stop=(kc == C - 1),
                       waits=[t_wv, xn_done[t]] if kc == 0 else (),
                       bank=bk if kc == 0 else None,
                       inc=(s_tn if kc == C - 1 else None))
                tok_b = (s_tn, EM.cnt["s_tn"])
                wl2 = [tok_b, t_bcl]
                if t == 0 and mo == 0:
                    wl2.append(v1_guard)
                if mo % 2 == 0:
                    t_v1 = EM.emit(
                        "scalar",
                        lambda e, mo=mo, bk=bk: e.activation(
                            v1_bf[:, mo, :], pb[bk][:], Act.Identity,
                            bias=bcols_s[:, 2, mo:mo + 1], scale=1.0),
                        waits=wl2, inc=s_sc)
                else:
                    t_v1 = EM.emit(
                        "vector",
                        lambda e, mo=mo, bk=bk: e.tensor_scalar(
                            v1_bf[:, mo, :], pb[bk][:],
                            bcols_s[:, 2, mo:mo + 1], None, Alu.add),
                        waits=wl2, inc=s_vc)
                read_bank(bk, t_v1)
                if mo % 2 == 1:
                    wst_free[wi] = tok_b
                if pend is not None:
                    emit_combine(pend, mo)
            if pend is not None:
                last_lnr_rd = pend["last_lnr"]
                last_ctx = pend
                pend = None
            t_mix = None
            for cch in range(C):
                wi = (cch // 2) % 2
                if cch % 2 == 0:
                    slab = N_WV20 + cch // 2
                    t_wv2 = dma("sync", s_ws[wi], wst[wi][:], c_w[slab],
                                waits=[cache_tok[slab], wst_free[wi]])
                bk = (0, 1, 4, 5)[cch % 4]
                for kc in range(C):
                    mm(pb[bk][:],
                       wst[wi][:, kc * 256 + (cch % 2) * 128:
                               kc * 256 + (cch % 2 + 1) * 128],
                       v1_bf[:, kc, :],
                       start=(kc == 0), stop=(kc == C - 1),
                       waits=[t_wv2, t_v1] if kc == 0 else (),
                       bank=bk if kc == 0 else None,
                       inc=(s_tn if kc == C - 1 else None))
                tok_b = (s_tn, EM.cnt["s_tn"])
                bx = 2 + cch % 2
                tok_x = mm(pb[bx][:], hmask_r[:, cch * 128:(cch + 1) * 128],
                           invq_r[:, tsl], True, True,
                           waits=[t_iqr, t_hmr, bank_guard[bx]], inc=s_tn)
                pp = cch % 2
                wl_qx = [tok_x, vpp_free[pp]]
                if t == 0 and cch < 2:
                    wl_qx.append(v1_guard)
                t_qx = EM.emit(
                    "scalar",
                    lambda e, pp=pp, bx=bx: e.copy(iqx[pp][:], pb[bx][:]),
                    waits=wl_qx, inc=s_sc)
                read_bank(bx, t_qx)
                t_vpp = EM.emit(
                    "vector",
                    lambda e, pp=pp, bk=bk, cch=cch: e.scalar_tensor_tensor(
                        v_pp[pp][:], pb[bk][:], bcols_s[:, 3, cch:cch + 1],
                        iqx[pp][:], Alu.add, Alu.mult),
                    waits=[tok_b, t_qx, t_bcl, vpp_free[pp]], inc=s_vc)
                read_bank(bk, t_vpp)
                bs = 6 + cch % 2
                tok_vs = mm(pb[bs][:], BDS[:, cch, :], v_pp[pp][:], True, True,
                            waits=[t_bds, t_vpp, bank_guard[bs]], inc=s_tn)
                vpp_free[pp] = tok_vs
                t_mix = EM.emit(
                    "scalar",
                    lambda e, cch=cch, bs=bs: e.copy(mix_bf[:, cch, :], pb[bs][:]),
                    waits=[tok_vs], inc=s_sc)
                read_bank(bs, t_mix)
                if cch % 2 == 1:
                    wst_free[wi] = tok_b
            t_y = None
            tok_ss = None
            for mo in range(C):
                wi = (mo // 2) % 2
                if mo % 2 == 0:
                    slab = N_WO0 + mo // 2
                    t_wo = dma("sync", s_ws[wi], wst[wi][:], c_w[slab],
                               waits=[cache_tok[slab], wst_free[wi]])
                bk = (0, 1, 2, 3)[mo % 4]
                for kc in range(C):
                    mm(pb[bk][:],
                       wst[wi][:, kc * 256 + (mo % 2) * 128:
                               kc * 256 + (mo % 2 + 1) * 128],
                       mix_bf[:, kc, :],
                       start=(kc == 0), stop=(kc == C - 1),
                       waits=[t_wo, t_mix] if kc == 0 else (),
                       bank=bk if kc == 0 else None,
                       inc=(s_tn if kc == C - 1 else None))
                tok_b = (s_tn, EM.cnt["s_tn"])
                wl2 = [tok_b, t_bcl]
                if t == 0 and mo == 0:
                    wl2 += [v1_guard]
                t_y = EM.emit(
                    "vector",
                    lambda e, mo=mo, bk=bk, tsl=tsl: e.scalar_tensor_tensor(
                        y_t[:, mo, :], pb[bk][:], bcols_s[:, 4, mo:mo + 1],
                        xn_bf[:, mo, tsl], Alu.add, Alu.add),
                    waits=wl2, inc=s_vc)
                read_bank(bk, t_y)
                if mo % 2 == 1:
                    wst_free[wi] = tok_b
                mm(pb[4][0:1, :], cones_r[:], y_t[:, mo, :],
                   start=(mo == 0), stop=(mo == C - 1),
                   waits=[t_y], bank=4 if mo == 0 else None)
                pp = mo % 2
                t_ysq = EM.emit(
                    "scalar",
                    lambda e, mo=mo, pp=pp: e.activation(
                        ysq[pp][:], y_t[:, mo, :], Act.Square),
                    waits=[t_y, ysq_free[pp]], inc=s_sc)
                tok_ss = mm(pb[5][0:1, :], cones_r[:], ysq[pp][:],
                            start=(mo == 0), stop=(mo == C - 1), waits=[t_ysq],
                            bank=5 if mo == 0 else None, inc=s_tn)
                ysq_free[pp] = tok_ss
            t_row2 = vchain("vector", s_vc, [
                lambda e: e.tensor_scalar(lnf[0:1, 0, :], pb[4][0:1, :], 1.0 / D, None, Alu.mult),
                lambda e: e.tensor_tensor(lnf[0:1, 1, :], lnf[0:1, 0, :], lnf[0:1, 0, :], Alu.mult),
                lambda e: e.scalar_tensor_tensor(lnf[0:1, 1, :], pb[5][0:1, :], 1.0 / D,
                                                 lnf[0:1, 1, :], Alu.mult, Alu.subtract),
                lambda e: e.tensor_scalar(lnf[0:1, 1, :], lnf[0:1, 1, :], EPS_LN2, None, Alu.add),
                lambda e: e.reciprocal(lnf[0:1, 1, :], lnf[0:1, 1, :]),
            ], waits=[tok_ss, last_lnr_rd])
            read_bank(4, t_row2)
            read_bank(5, t_row2)
            t_rstd2 = EM.emit(
                "scalar", lambda e: e.activation(lnr_r[0:1, 0, :], lnf[0:1, 1, :], Act.Sqrt),
                waits=[t_row2], inc=s_sc)
            t_lncp2 = EM.emit(
                "vector",
                lambda e: e.scalar_tensor_tensor(
                    lnr_r[0:1, 1, :], lnf[0:1, 0, :], -1.0, lnr_r[0:1, 0, :],
                    Alu.mult, Alu.mult),
                waits=[t_rstd2], inc=s_vc)
            pend = {"par": par, "tsl": tsl, "t_lncp2": t_lncp2, "t_x2": t_x2}
        for mo in range(C):
            emit_combine(pend, mo)
        last_lnr_rd = pend["last_lnr"]
        t_out_last = pend["t_out"]

        if debug:
            t_d = dma("gpsimd", s_gd, dbg["d_sct"], SCt[:], waits=[t_rtr])
            t_d = dma("gpsimd", s_gd, dbg["d_sfc"], sfc[:], waits=[t_rtr])
            t_d = dma("gpsimd", s_gd, dbg["d_selw"], selw[:], waits=[t_rtr])
            t_d = dma("gpsimd", s_gd, dbg["d_cwtm"], cw_tm[:], waits=[t_rtr])
            t_d = dma("gpsimd", s_gd, dbg["d_spk"], S_pk[:], waits=[t_sx])
            t_d = dma("gpsimd", s_gd, dbg["d_sfl"], S_fl[:], waits=[t_sfl])
            t_d = dma("gpsimd", s_gd, dbg["d_xn"], xn_bf[:], waits=[xn_done[-1]])
            t_d = dma("gpsimd", s_gd, dbg["d_g"], g_bf[:], waits=[g_written[-1]])
            t_d = dma("gpsimd", s_gd, dbg["d_iq"], invq_tm[:], waits=[t_invq])
            t_d = dma("gpsimd", s_gd, dbg["d_cw"], cw_r[:].bitcast(F32), waits=[t_cwr])
            t_d = dma("gpsimd", s_gd, dbg["d_v1"], v1_bf[:], waits=[t_v1])
            t_d = dma("gpsimd", s_gd, dbg["d_mix"], mix_bf[:], waits=[t_mix])
            EM.wait("gpsimd", t_d)
        EM.wait("scalar", (s_ob[0], EM.cnt.get("s_ob0", 0)),
                (s_ob[1], EM.cnt.get("s_ob1", 0)))

        block = es.enter_context(nc.Block())

        @block.sync
        def _(sync):
            for fn in EM.prog["sync"]:
                fn(sync)

        @block.scalar
        def _(scalar):
            for fn in EM.prog["scalar"]:
                fn(scalar)

        @block.vector
        def _(vector):
            for fn in EM.prog["vector"]:
                fn(vector)

        @block.tensor
        def _(tensor):
            for fn in EM.prog["tensor"]:
                fn(tensor)

        @block.gpsimd
        def _(gpsimd):
            for fn in EM.prog["gpsimd"]:
                fn(gpsimd)

    return nc


_CACHE = {}


def _pack_fm(w):
    return np.ascontiguousarray(w.reshape(C, 128, -1).transpose(1, 0, 2))


def _slabs_dout(w):
    a = w.reshape(C, 128, 1024).transpose(1, 0, 2)
    return [np.ascontiguousarray(a[:, :, q * 256:(q + 1) * 256].reshape(128, 2048))
            for q in range(4)]


def _build_inputs(inp):
    x = np.asarray(inp["L0_attn_x"], dtype=np.float32)
    B, L, _ = x.shape
    assert (B, L) == (2, 8192)

    w1 = np.empty((16, 128, 2048), np.float32)
    for wi, nm in enumerate(("Wq", "Wk", "Wq2", "Wk2")):
        a = np.asarray(inp[f"L0_attn_{nm}"], np.float32)
        flat = a.reshape(C, 128, 1024).transpose(1, 0, 2).reshape(128, 8192)
        for q in range(4):
            w1[wi * 4 + q] = flat[:, q * 2048:(q + 1) * 2048]

    wffn = np.empty((N_WSLAB, 128, 2048), np.float32)
    up = np.asarray(inp["L0_ffn_up_weights"], np.float32)
    dn = np.asarray(inp["L0_ffn_down_weights"], np.float32)
    su = np.asarray(inp["L0_ffn_shared_up_weight"], np.float32)
    sd = np.asarray(inp["L0_ffn_shared_down_weight"], np.float32)
    for e_i in range(E):
        upT = up[e_i].T
        a = upT.reshape(C, 128, 512).transpose(1, 0, 2)
        for hf in range(2):
            wffn[N_UP0 + e_i * 2 + hf] = a[:, :, hf * 256:(hf + 1) * 256].reshape(128, 2048)
    for j in range(C):
        for eh in range(2):
            cols = []
            for e4 in range(4):
                dnT = dn[eh * 4 + e4].T
                blk = dnT.reshape(4, 128, 1024)[:, :, j * 128:(j + 1) * 128]
                cols.append(blk.transpose(1, 0, 2).reshape(128, 512))
            wffn[N_DN0 + j * 2 + eh] = np.concatenate(cols, axis=1)
    suT = su.T
    for j2 in range(E):
        cols = []
        for j in (2 * j2, 2 * j2 + 1):
            blk = suT.reshape(C, 128, 2048)[:, :, j * 128:(j + 1) * 128]
            cols.append(blk.transpose(1, 0, 2).reshape(128, 1024))
        wffn[N_SU0 + j2] = np.concatenate(cols, axis=1)
    sdT = sd.T
    for j in range(C):
        blk = sdT.reshape(SIC, 128, 1024)[:, :, j * 128:(j + 1) * 128]
        wffn[N_SD0 + j] = blk.transpose(1, 0, 2).reshape(128, 2048)
    for base, nm in ((N_WV0, "Wv"), (N_WV20, "Wv2"), (N_WO0, "Wo")):
        for q, slab in enumerate(_slabs_dout(np.asarray(inp[f"L0_attn_{nm}"], np.float32))):
            wffn[base + q] = slab

    brows = np.zeros((1, 7, 512), np.float32)
    brows[0, 3] = np.asarray(inp["L0_attn_bq2"], np.float32)[0:512]
    brows[0, 4] = np.asarray(inp["L0_attn_bq2"], np.float32)[512:1024]
    brows[0, 5] = np.asarray(inp["L0_attn_bk2"], np.float32)[0:512]
    brows[0, 6] = np.asarray(inp["L0_attn_bk2"], np.float32)[512:1024]
    bcols = np.zeros((128, 7, C), np.float32)
    for bi, nm in enumerate(("bq", "bk", "bv", "bv2", "bo", "gamma", "beta")):
        bcols[:, bi, :] = np.asarray(
            inp[f"L0_attn_{nm}"], np.float32).reshape(C, 128).T
    gatet = _pack_fm(np.ascontiguousarray(
        np.asarray(inp["L0_ffn_gate_weight"], np.float32).T))
    gbias = np.ascontiguousarray(np.broadcast_to(
        np.asarray(inp["L0_ffn_gate_bias"], np.float32), (128, H, E)))
    hmask = np.zeros((H, 2 * D), np.float32)
    for h in range(H):
        hmask[h, h * DH:(h + 1) * DH] = 1.0
    for e_i in range(E):
        hmask[e_i, D + e_i * 128:D + (e_i + 1) * 128] = 1.0

    shared = {
        "w1": w1, "wffn": wffn, "brows": brows, "bcols": bcols,
        "gatet": gatet, "gbias": gbias,
        "ident": np.eye(128, dtype=np.float32),
        "hmask": hmask,
    }
    x_fm = x.transpose(0, 2, 1)
    in_maps = []
    for core in range(8):
        b = core // 4
        sl = x_fm[b, :, (core % 4) * TC:(core % 4 + 1) * TC]
        xc = np.ascontiguousarray(sl.reshape(C, 128, TC).transpose(1, 0, 2))
        m = dict(shared)
        m["x"] = xc
        in_maps.append(m)
    return in_maps


def kernel(**inputs):
    if "nc" not in _CACHE:
        _CACHE["nc"] = build_program()
    nc = _CACHE["nc"]
    in_maps = _build_inputs(inputs)
    res = run_bass_kernel_spmd(nc, in_maps, core_ids=list(range(8)))
    out = np.empty((2, 8192, D), np.float32)
    for core in range(8):
        oc = np.asarray(res.results[core]["out"])
        fm = oc.transpose(1, 0, 2).reshape(D, TC)
        b = core // 4
        t0 = (core % 4) * TC
        out[b, t0:t0 + TC, :] = fm.T
    return out


# version 21
# speedup vs baseline: 1.0253x; 1.0253x over previous
import contextlib

import numpy as np

import concourse.bass as bass
import concourse.mybir as mybir
from concourse.bass_utils import run_bass_kernel_spmd

F32 = mybir.dt.float32
F32R = mybir.dt.float32r
BF16 = mybir.dt.bfloat16
Alu = mybir.AluOpType
Act = mybir.ActivationFunctionType
AxX = mybir.AxisListType.X

D = 1024
C = 8
TC = 2048
TT = 512
NT = TC // TT
NG = TT // 128
H = 16
DH = 64
E = 8
SIC = 16
EPS_LN1 = 1e-5
EPS_LN2 = 1e-3
EPS_QK = 1e-6
RSCALE = 2.5

ENGS = ("sync", "scalar", "vector", "tensor", "gpsimd")

SB_TOP = 224 * 1024 - 64
RA = SB_TOP - 64 * 1024
RB = RA - 32 * 1024
RC = RB - 32 * 1024

N_UP0 = 0
N_DN0 = 16
N_SU0 = 32
N_SD0 = 40
N_WV0 = 48
N_WV20 = 52
N_WO0 = 56
N_WSLAB = 60


class _Em:

    def __init__(self):
        self.prog = {e: [] for e in ENGS}
        self.cnt = {}
        self.hwm = {}

    def emit(self, eng, fn, waits=(), inc=None):
        wl = []
        for tok in waits:
            if tok is None:
                continue
            s, v = tok
            key = (eng, s.name)
            if self.hwm.get(key, -1) >= v:
                continue
            self.hwm[key] = v
            wl.append((s, v))
        tok = None
        if inc is not None:
            sem, amt = inc if isinstance(inc, tuple) else (inc, 1)
            self.cnt[sem.name] = self.cnt.get(sem.name, 0) + amt
            tok = (sem, self.cnt[sem.name])
            inc = (sem, amt)

        def run(engine, fn=fn, wl=wl, inc=inc):
            for s, v in wl:
                engine.wait_ge(s, v)
            inst = fn(engine)
            if inc is not None:
                inst.then_inc(inc[0], inc[1])

        self.prog[eng].append(run)
        return tok

    def wait(self, eng, *toks):
        wl = []
        for tok in toks:
            if tok is None:
                continue
            s, v = tok
            key = (eng, s.name)
            if self.hwm.get(key, -1) >= v:
                continue
            self.hwm[key] = v
            wl.append((s, v))
        if wl:
            def run(engine, wl=wl):
                for s, v in wl:
                    engine.wait_ge(s, v)
            self.prog[eng].append(run)


def build_program(cc_groups=((0, 1, 2, 3), (4, 5, 6, 7)), debug=False):
    nc = bass.Bass("TRN2", target_bir_lowering=False,
                   detect_race_conditions=False)

    di = {}
    def din(name, shape):
        di[name] = nc.dram_tensor(name, list(shape), F32, kind="ExternalInput").ap()
    din("x", (128, C, TC))
    din("w1", (16, 128, 2048))
    din("wffn", (N_WSLAB, 128, 2048))
    din("brows", (1, 7, 512))
    din("bcols", (128, 7, C))
    din("gatet", (128, C, E))
    din("gbias", (128, H, E))
    din("ident", (128, 128))
    din("hmask", (H, 2 * D))
    out_d = nc.dram_tensor("out", [128, C, TC], F32, kind="ExternalOutput").ap()
    dbg = {}
    if debug:
        for nm, shp, dt_ in (("d_sct", (128, H, E), F32), ("d_sfc", (128, H, E), F32),
                             ("d_selw", (128, H, E), F32), ("d_cwtm", (128, H, E), F32),
                             ("d_spk", (128, 512), F32), ("d_sfl", (128, 512), F32),
                             ("d_xn", (128, C, TC), BF16), ("d_g", (128, C, TC), BF16),
                             ("d_iq", (128, H * H), F32), ("d_cw", (E, TC), F32),
                             ("d_v1", (128, C, TT), BF16), ("d_mix", (128, C, TT), BF16)):
            dbg[nm] = nc.dram_tensor(nm, list(shp), dt_, kind="ExternalOutput").ap()

    c_w = nc.dram_tensor("c_w", [N_WSLAB, 128, 2048], BF16).ap()
    cc_in = nc.dram_tensor("cc_in", [128, 512], F32).ap()
    cc_out = nc.dram_tensor("cc_out", [128, 512], F32).ap()

    with contextlib.ExitStack() as es:
        def sbn(name, shape, dtype):
            return es.enter_context(nc.sbuf_tensor(name, list(shape), dtype))

        def sba(name, shape, dtype, off):
            return nc.alloc_sbuf_tensor_at(name, list(shape), dtype, offset=off)

        xn_bf = sbn("xn_bf", (128, C, TC), BF16)
        lnr_r = sbn("lnr_r", (1, 7, 512), F32R)
        lnf = sbn("lnf", (1, 2, 512), F32)
        bcols_s = sbn("bcols_s", (128, 7, C), F32)
        SCt = sbn("SCt", (128, H, E), F32)
        sfc = sbn("sfc", (128, H, E), F32)
        msk = sbn("msk", (128, H, E), F32)
        selw = sbn("selw", (128, H, E), F32)
        cw_tm = sbn("cw_tm", (128, H, E), F32)
        rt = [sbn(f"rt{i}", (128, H), F32) for i in range(8)]
        cw_r = sbn("cw_r", (E, TC), F32R)
        invq_tm = sbn("invq_tm", (128, H * H), F32)
        S_pk = sbn("S_pk", (128, 512), F32)
        S_fl = sbn("S_fl", (128, 512), F32)
        BDS = sbn("BDS", (128, C, 128), BF16)
        gatet_s = sbn("gatet_s", (128, C, E), F32)
        gbias_s = sbn("gbias_s", (128, H, E), F32)
        cones_s = sbn("cones_s", (128, 1), F32)
        cones_r = sbn("cones_r", (128, 1), F32R)
        ones128_r = sbn("ones128_r", (1, 128), F32R)
        ident_s = sbn("ident_s", (128, 128), F32)
        hmask_r = sbn("hmask_r", (H, 2 * D), F32R)

        wslot = [sba(f"wslot{i}", (128, C, D), BF16, RA + i * 16384)
                 for i in range(4)]
        r2s = sba("r2s", (128, E * 4, TT), BF16, RA)
        u_bf = sba("u_bf", (128, SIC, TT), BF16, RA + 32768)
        fr_t = [sba(f"fr_t{i}", (128, TT), F32, RA + 49152 + i * 2048)
                for i in range(2)]
        fs_t = [sba(f"fs_t{i}", (128, TT), F32, RA + 53248 + i * 2048)
                for i in range(2)]
        y_t = sba("y_t", (128, C, TT), F32R, RA)
        v1_bf = sba("v1_bf", (128, C, TT), BF16, RA + 16384)
        mix_bf = sba("mix_bf", (128, C, TT), BF16, RA + 24576)
        wst = [sba(f"wst{i}", (128, 2048), BF16, RA + 32768 + i * 4096)
               for i in range(2)]
        ysq = [sba(f"ysq{i}", (128, TT), F32R, RA + 40960 + i * 2048)
               for i in range(2)]
        s_gs = sba("s_gs", (128, TT), F32, RA + 45056)
        s_t1 = sba("s_t1", (128, TT), F32, RA + 47104)
        v_pp = [sba(f"v_pp{i}", (128, TT), BF16, RA + 49152 + i * 1024)
                for i in range(2)]
        invq_r = sba("invq_r", (H, TC), F32R, RA + 57344)
        iqx = [sba(f"iqx{i}", (128, TT), F32, RA + 51200 + i * 2048)
               for i in range(2)]

        xtile = [sba(f"xtile{i}", (128, C, TT), F32, RB + i * 16384)
                 for i in range(2)]
        q1_bf = sba("q1_bf", (128, C, TT), BF16, RB)
        k1_bf = sba("k1_bf", (128, C, TT), BF16, RB + 8192)
        q_tm = sba("q_tm", (128, NG, D), BF16, RB + 16384)
        k_tm = sba("k_tm", (128, NG, D), BF16, RB + 24576)
        fstr = [sba(f"fstr{i}", (128, 2048), BF16, RB + i * 4096)
                for i in range(8)]

        wstg = [sba(f"wstg{i}", (128, 2048), F32, RC + i * 8192)
                for i in range(2)]
        wcvt = [sba(f"wcvt{i}", (128, 2048), BF16, RC + 16384 + i * 4096)
                for i in range(2)]
        xsq = [sba(f"xsq{i}", (128, TT), F32R, RC + 24576 + i * 2048)
               for i in range(2)]
        xnf = [sba(f"xnf{i}", (128, TT), F32, RC + 28672 + i * 2048)
               for i in range(2)]
        cw_fm = sba("cw_fm", (E, TC), F32, RC + 24576)
        invq_fm = sba("invq_fm", (H, TC), F32, RC + 24576)
        g_bf = sba("g_bf", (128, C, TC), BF16, RC)
        brows_s = sba("brows_s", (1, 7, 512), F32, RC)
        hmask_s = sba("hmask_s", (H, 2 * D), F32, RC + 16384)
        elu_m = [sba(f"elu_m{i}", (128, 512), F32, RC + 24576 + i * 2048)
                 for i in range(2)]
        elu_rt = [sba(f"elu_rt{i}", (128, 512), F32, RC + 28672 + i * 2048)
                  for i in range(2)]
        qkf = sbn("qkf", (128, 512), BF16)

        pb = [es.enter_context(nc.psum_tensor(f"pb{i}", [128, 512], F32))
              for i in range(8)]

        s_in = es.enter_context(nc.semaphore("s_in"))
        s_x = [es.enter_context(nc.semaphore(f"s_x{i}")) for i in range(2)]
        s_wp = [es.enter_context(nc.semaphore(f"s_wp{i}")) for i in range(2)]
        s_cw = [es.enter_context(nc.semaphore(f"s_cw{i}")) for i in range(2)]
        s_fs = [es.enter_context(nc.semaphore(f"s_fs{i}")) for i in range(8)]
        s_ws = [es.enter_context(nc.semaphore(f"s_ws{i}")) for i in range(2)]
        s_ob = [es.enter_context(nc.semaphore(f"s_ob{i}")) for i in range(2)]
        s_scc = es.enter_context(nc.semaphore("s_scc"))
        s_gd = es.enter_context(nc.semaphore("s_gd"))
        s_gd2 = es.enter_context(nc.semaphore("s_gd2"))
        s_tn = es.enter_context(nc.semaphore("s_tn"))
        s_vc = es.enter_context(nc.semaphore("s_vc"))
        s_sc = es.enter_context(nc.semaphore("s_sc"))
        s_cc = es.enter_context(nc.semaphore("s_cc"))

        assert nc.sbuf_bytes_remaining >= SB_TOP - RC, (
            "P0 allocations overflow into overlay regions: remaining="
            f"{nc.sbuf_bytes_remaining}, need >= {SB_TOP - RC}")

        EM = _Em()
        bank_guard = [None] * 8

        def dma(q, sem, dst, src, waits=()):
            prev = EM.cnt.get(sem.name, 0)
            w = list(waits) + ([(sem, prev)] if prev else [])
            return EM.emit(q, lambda e, d=dst, s=src: e.dma_start(out=d, in_=s),
                           waits=w, inc=(sem, 16))

        def mm(out, lhsT, rhs, start, stop, waits=(), inc=None, bank=None):
            w = list(waits)
            if start and bank is not None and bank_guard[bank] is not None:
                w.append(bank_guard[bank])
            return EM.emit(
                "tensor",
                lambda e, o=out, l=lhsT, r=rhs, st=start, sp=stop:
                    e.matmul(o, l, r, start=st, stop=sp),
                waits=w, inc=inc)

        def read_bank(bank, tok):
            bank_guard[bank] = tok

        def vchain(eng, sem, fns, waits=()):
            tok = None
            for i, fn in enumerate(fns):
                w = list(waits) if i == 0 else [tok]
                tok = EM.emit(eng, fn, waits=w, inc=sem)
            return tok

        t_ident = dma("sync", s_in, ident_s[:], di["ident"])
        t_avl = dma("sync", s_in, brows_s[:], di["brows"])
        t_hml = dma("sync", s_in, hmask_s[:], di["hmask"])
        t_gatet = dma("sync", s_in, gatet_s[:], di["gatet"])
        t_gbias = dma("sync", s_in, gbias_s[:], di["gbias"])
        t_bcl = dma("sync", s_in, bcols_s[:], di["bcols"])
        t_avr = EM.emit("scalar", lambda e: e.copy(lnr_r[0:1, 0:7, :], brows_s[:]),
                        waits=[t_avl], inc=s_sc)
        t_hmr = EM.emit("scalar", lambda e: e.copy(hmask_r[:], hmask_s[:]),
                        waits=[t_hml], inc=s_sc)

        def init_consts(e):
            e.memset(cones_s[:], 1.0)
            return e.memset(lnf[0:1, 0, :], 1.0)
        t_onesf = EM.emit("vector", init_consts, waits=[], inc=s_vc)

        def init_consts2(e):
            e.copy(ones128_r[:], lnf[0:1, 0, 0:128])
            e.copy(cones_r[:], cones_s[:])
            return e.copy(lnr_r[0:1, 2, :], lnf[0:1, 0, :])
        t_ones = EM.emit("scalar", init_consts2, waits=[t_onesf, t_avr], inc=s_sc)
        t_cones = t_ones
        t_o128 = t_ones

        stg_free = [None, None]
        cvt_free = [None, None]
        piece_i = [0]
        cache_tok = {}

        def w_piece(src_ap, dst_sbuf=None, cache_slab=None):
            i = piece_i[0] % 2
            ceng = (("scalar", s_sc), ("vector", s_vc))[piece_i[0] % 2]
            piece_i[0] += 1
            w0 = [t_avr, t_hmr] if piece_i[0] <= 2 else []
            t_l = dma("sync", s_wp[i], wstg[i][:], src_ap,
                      waits=[stg_free[i]] + w0)
            use_tc = ceng[0] == "vector"
            if dst_sbuf is not None:
                t_c = EM.emit(ceng[0],
                              lambda e, i=i, d=dst_sbuf, u=use_tc: e.tensor_copy(d, wstg[i][:])
                              if u else e.copy(d, wstg[i][:]),
                              waits=[t_l], inc=ceng[1])
                stg_free[i] = t_c
                return t_c
            wcv = [t_hmr] if piece_i[0] <= 18 else []
            t_c = EM.emit(ceng[0],
                          lambda e, i=i, u=use_tc: e.tensor_copy(wcvt[i][:], wstg[i][:])
                          if u else e.copy(wcvt[i][:], wstg[i][:]),
                          waits=[t_l, cvt_free[i]] + wcv, inc=ceng[1])
            stg_free[i] = t_c
            t_s = dma("scalar", s_cw[i], c_w[cache_slab], wcvt[i][:], waits=[t_c])
            cvt_free[i] = t_s
            cache_tok[cache_slab] = t_s
            return t_s

        w1_tok = {}
        PIECES = []
        for wi in range(4):
            for q in range(4):
                dst = wslot[wi][:, 2 * q:2 * q + 2, :].rearrange("p a b -> p (a b)")
                PIECES.append(("w1", wi, di["w1"][4 * wi + q], dst, None))
        for s in range(N_WSLAB):
            PIECES.append(("wffn", None, di["wffn"][s], None, s))
        piece_cursor = [0]

        def emit_pieces(n):
            while n > 0 and piece_cursor[0] < len(PIECES):
                kind, wi, src_ap, dst, slab = PIECES[piece_cursor[0]]
                piece_cursor[0] += 1
                tok = w_piece(src_ap, dst_sbuf=dst, cache_slab=slab)
                if kind == "w1":
                    w1_tok[wi] = tok
                n -= 1


        x_free = [(), ()]
        xn_done = []
        t_sig = None
        xsq_free = [None, None]
        xnf_free = [None, None]
        xnf_sc_free = [None, None]
        last_lnr_rd = None

        x_tok = [None] * NT
        x_tok[0] = dma("sync", s_x[0], xtile[0][:], di["x"][:, :, 0:TT])
        for t in range(NT):
            par = t % 2
            tsl = slice(t * TT, (t + 1) * TT)
            if t + 1 < NT:
                x_tok[t + 1] = dma("sync", s_x[(t + 1) % 2], xtile[(t + 1) % 2][:],
                                   di["x"][:, :, (t + 1) * TT:(t + 2) * TT],
                                   waits=list(x_free[(t + 1) % 2]))
            t_x = x_tok[t]
            t_sum = None
            for c in range(C):
                t_sum = mm(pb[0][0:1, :], cones_s[:], xtile[par][:, c, :],
                           start=(c == 0), stop=(c == C - 1),
                           waits=[t_x, t_cones] if c == 0 else (),
                           bank=0 if c == 0 else None,
                           inc=(s_tn if c == C - 1 else None))
            tok_ss = None
            last_tsq = None
            for c in range(C):
                pp = c % 2
                last_tsq = EM.emit(
                    "scalar",
                    lambda e, c=c, pp=pp, par=par: e.activation(
                        xsq[pp][:], xtile[par][:, c, :], Act.Square),
                    waits=[t_x, xsq_free[pp]], inc=s_sc)
                tok_ss = mm(pb[5][0:1, :], cones_r[:], xsq[pp][:],
                            start=(c == 0), stop=(c == C - 1), waits=[last_tsq],
                            bank=5 if c == 0 else None, inc=s_tn)
                xsq_free[pp] = tok_ss

            t_row = vchain("vector", s_vc, [
                lambda e: e.tensor_scalar(lnf[0:1, 0, :], pb[0][0:1, :], 1.0 / D, None, Alu.mult),
                lambda e: e.tensor_tensor(lnf[0:1, 1, :], lnf[0:1, 0, :], lnf[0:1, 0, :], Alu.mult),
                lambda e: e.scalar_tensor_tensor(lnf[0:1, 1, :], pb[5][0:1, :], 1.0 / D,
                                                 lnf[0:1, 1, :], Alu.mult, Alu.subtract),
                lambda e: e.tensor_scalar(lnf[0:1, 1, :], lnf[0:1, 1, :], EPS_LN1, None, Alu.add),
            ], waits=[tok_ss, last_lnr_rd])
            read_bank(0, t_row)
            read_bank(5, t_row)
            t_ln = EM.emit(
                "scalar", lambda e: e.activation(lnf[0:1, 1, :], lnf[0:1, 1, :], Act.Ln),
                waits=[t_row], inc=s_sc)
            t_rstd = EM.emit(
                "scalar", lambda e: e.activation(lnr_r[0:1, 0, :], lnf[0:1, 1, :],
                                                 Act.Exp, scale=-0.5),
                waits=[t_ln], inc=s_sc)
            t_lncp = EM.emit(
                "vector",
                lambda e: e.scalar_tensor_tensor(
                    lnr_r[0:1, 1, :], lnf[0:1, 0, :], -1.0, lnr_r[0:1, 0, :],
                    Alu.mult, Alu.mult),
                waits=[t_rstd], inc=s_vc)
            t_A1 = mm(pb[1][:], ones128_r[:], lnr_r[0:1, 0, :], True, True,
                      waits=[t_lncp, t_o128], bank=1, inc=s_tn)
            t_B1 = mm(pb[2][:], ones128_r[:], lnr_r[0:1, 1, :], True, True,
                      bank=2, inc=s_tn)
            last_lnr_rd = t_B1
            last_gmm = None
            t_xnbf = None
            for c in range(C):
                pp = c % 2
                def xn_tt(e, c=c, pp=pp, par=par):
                    e.tensor_tensor(xnf[pp][:], xtile[par][:, c, :], pb[1][:], Alu.mult)
                    return e.tensor_tensor(xnf[pp][:], xnf[pp][:], pb[2][:], Alu.add)
                t_xnf = EM.emit("vector", xn_tt,
                                waits=[t_A1, t_B1, xnf_free[pp], xnf_sc_free[pp]],
                                inc=s_vc)
                t_xnbf = EM.emit(
                    "scalar",
                    lambda e, c=c, pp=pp, t=t: e.copy(
                        xn_bf[:, c, t * TT:(t + 1) * TT], xnf[pp][:]),
                    waits=[t_xnf], inc=s_sc)
                for g in range(NG):
                    last_gmm = mm(
                        pb[3][:, g * 8:(g + 1) * 8],
                        xnf[pp][:, g * 128:(g + 1) * 128],
                        gatet_s[:, c, :],
                        start=(c == 0 and g == 0),
                        stop=(c == C - 1 and g == NG - 1),
                        waits=[t_xnf, t_gatet] if g == 0 else (),
                        bank=3 if (c == 0 and g == 0) else None,
                        inc=(s_tn if g == NG - 1 else None))
                xnf_free[pp] = last_gmm
                xnf_sc_free[pp] = t_xnbf
            xn_done.append(t_xnbf)
            x_free[par] = (t_sum, last_tsq, t_xnf)
            t_sig = EM.emit(
                "scalar",
                lambda e, t=t: e.activation(
                    SCt[:, t * NG:(t + 1) * NG, :].rearrange("p a b -> p (a b)"),
                    pb[3][:, 0:32], Act.Sigmoid),
                waits=[last_gmm], inc=s_sc)
            read_bank(3, t_sig)
            emit_pieces(4)
            read_bank(1, t_xnf)
            read_bank(2, t_xnf)

        rf = [lambda e: e.tensor_tensor(sfc[:], SCt[:], gbias_s[:], Alu.add)]
        for gi in range(2):
            base = gi * 4
            first = True
            for a in range(4):
                for b in range(a + 1, 4):
                    rf.append(lambda e, base=base, a=a, b=b: e.tensor_tensor(
                        rt[2][:], sfc[:, :, base + a], sfc[:, :, base + b], Alu.add))
                    if first:
                        rf.append(lambda e, gi=gi: e.tensor_copy(rt[gi][:], rt[2][:]))
                        first = False
                    else:
                        rf.append(lambda e, gi=gi: e.tensor_tensor(
                            rt[gi][:], rt[gi][:], rt[2][:], Alu.max))
        rf.append(lambda e: e.tensor_tensor(rt[3][:], rt[0][:], rt[1][:], Alu.is_ge))
        rf.append(lambda e: e.tensor_scalar(rt[4][:], rt[3][:], -1.0, 1.0, Alu.mult, Alu.add))
        for x in range(E):
            gm = rt[3] if x < 4 else rt[4]
            rf.append(lambda e, x=x, gm=gm: e.tensor_tensor(
                msk[:, :, x], sfc[:, :, x], gm[:], Alu.mult))
        rf.append(lambda e: e.tensor_tensor(rt[5][:], msk[:, :, 0], msk[:, :, 1], Alu.max))
        for x in range(2, E):
            rf.append(lambda e, x=x: e.tensor_tensor(rt[5][:], rt[5][:], msk[:, :, x], Alu.max))
        for x in range(E):
            rf.append(lambda e, x=x: e.tensor_tensor(
                selw[:, :, x], msk[:, :, x], rt[5][:], Alu.is_ge))
        for x in range(E):
            rf.append(lambda e, x=x: e.scalar_tensor_tensor(
                cw_tm[:, :, x], selw[:, :, x], -1e30, msk[:, :, x], Alu.mult, Alu.add))
        rf.append(lambda e: e.tensor_tensor(rt[6][:], cw_tm[:, :, 0], cw_tm[:, :, 1], Alu.max))
        for x in range(2, E):
            rf.append(lambda e, x=x: e.tensor_tensor(rt[6][:], rt[6][:], cw_tm[:, :, x], Alu.max))
        for x in range(E):
            rf.append(lambda e, x=x: e.tensor_tensor(
                msk[:, :, x], cw_tm[:, :, x], rt[6][:], Alu.is_ge))
            rf.append(lambda e, x=x: e.tensor_tensor(
                selw[:, :, x], selw[:, :, x], msk[:, :, x], Alu.add))
        for x in range(E):
            rf.append(lambda e, x=x: e.tensor_tensor(
                selw[:, :, x], selw[:, :, x], SCt[:, :, x], Alu.mult))
        rf.append(lambda e: e.tensor_tensor(rt[7][:], selw[:, :, 0], selw[:, :, 1], Alu.add))
        for x in range(2, E):
            rf.append(lambda e, x=x: e.tensor_tensor(rt[7][:], rt[7][:], selw[:, :, x], Alu.add))
        rf.append(lambda e: e.tensor_scalar(rt[7][:], rt[7][:], 1e-20, None, Alu.add))
        rf.append(lambda e: e.reciprocal(rt[7][:], rt[7][:]))
        rf.append(lambda e: e.tensor_scalar(rt[7][:], rt[7][:], RSCALE, None, Alu.mult))
        for x in range(E):
            rf.append(lambda e, x=x: e.tensor_tensor(
                cw_tm[:, :, x], selw[:, :, x], rt[7][:], Alu.mult))
        t_rtr = vchain("vector", s_vc, rf, waits=[t_sig, t_gbias])


        s1_ev = {}
        s2_guard = {}
        eb_ctr = [0]
        elu_free = [None, None]
        qk_done = {}
        S_tile_tok = None
        qk_free = None
        t_S_last = None
        t_qk_read = None
        last_s2_mm = None
        t_red_last = None

        ho_B = list(x_free[0]) + list(x_free[1])
        first_B_ev = True
        first_B_tm = True
        first_B_elu = True
        for t in range(NT):
            tsl = slice(t * TT, (t + 1) * TT)
            for proj, wi1, dst1 in (("q", 0, q1_bf), ("k", 1, k1_bf)):
                for mo in range(C):
                    bk = mo % 4
                    wl = [w1_tok[wi1], xn_done[t], t_avr, t_ones]
                    for kc in range(C):
                        mm(pb[bk][:], wslot[wi1][:, kc, mo * 128:(mo + 1) * 128],
                           xn_bf[:, kc, tsl], start=(kc == 0), stop=(kc == C - 1),
                           waits=wl if kc == 0 else (), bank=bk if kc == 0 else None,
                           inc=(s_tn if kc == C - 1 else None))
                    bcol = 0 if proj == "q" else 1
                    tok_b = (s_tn, EM.cnt["s_tn"])
                    wl_ev = [tok_b, t_bcl] + (ho_B if first_B_ev else [])
                    first_B_ev = False
                    if mo % 2 == 0:
                        t_ev = EM.emit(
                            "scalar",
                            lambda e, d=dst1, mo=mo, bk=bk, bcol=bcol: e.activation(
                                d[:, mo, :], pb[bk][:], Act.Identity,
                                bias=bcols_s[:, bcol, mo:mo + 1], scale=1.0),
                            waits=wl_ev, inc=s_sc)
                    else:
                        t_ev = EM.emit(
                            "vector",
                            lambda e, d=dst1, mo=mo, bk=bk, bcol=bcol: e.tensor_scalar(
                                d[:, mo, :], pb[bk][:],
                                bcols_s[:, bcol, mo:mo + 1], None, Alu.add),
                            waits=wl_ev, inc=s_vc)
                    read_bank(bk, t_ev)
                    s1_ev[(proj, t)] = t_ev
                    emit_pieces(1)
            for g in range(NG):
                gi = t * NG + g
                tok_q_done = tok_k_done = None
                for proj, wi2, src1, dst_tm in (
                        ("q", 2, q1_bf, q_tm), ("k", 3, k1_bf, k_tm)):
                    brow = 3 if proj == "q" else 5
                    bks = ((2, 3), (4, 5), (0, 1))[(2 * g + (0 if proj == "q" else 1)) % 3]
                    for half in range(2):
                        bk = bks[half]
                        wl = [w1_tok[wi2], s1_ev[(proj, t)]]
                        for kc in range(C):
                            mm(pb[bk][:],
                               src1[:, kc, g * 128:(g + 1) * 128],
                               wslot[wi2][:, kc, half * 512:(half + 1) * 512],
                               start=(kc == 0), stop=False,
                               waits=wl if kc == 0 else (),
                               bank=bk if kc == 0 else None)
                        tok_b = mm(pb[bk][:], ones128_r[:],
                                   lnr_r[0:1, brow + half, :],
                                   start=False, stop=True, inc=s_tn)
                        last_s2_mm = tok_b
                        eb = eb_ctr[0] % 2
                        eb_ctr[0] += 1
                        wl_min = [tok_b, elu_free[eb]]
                        if first_B_elu:
                            wl_min += [xsq_free[0], xsq_free[1], xnf_free[0],
                                       xnf_free[1], xnf_sc_free[0], xnf_sc_free[1]]
                        t_min = EM.emit(
                            "vector",
                            lambda e, bk=bk, eb=eb: e.tensor_scalar_min(
                                elu_m[eb][:], pb[bk][:], 0.0),
                            waits=wl_min, inc=s_vc)
                        first_B_elu = False
                        t_rel = EM.emit(
                            "scalar",
                            lambda e, bk=bk, eb=eb: e.activation(
                                elu_rt[eb][:], pb[bk][:], Act.Relu),
                            waits=[t_min], inc=s_sc)
                        read_bank(bk, t_rel)
                        t_exp = EM.emit(
                            "scalar",
                            lambda e, eb=eb: e.activation(elu_m[eb][:], elu_m[eb][:], Act.Exp),
                            waits=[t_rel], inc=s_sc)
                        wl_tm = [t_exp, S_tile_tok] + (ho_B if first_B_tm else [])
                        first_B_tm = False
                        t_add = EM.emit(
                            "vector",
                            lambda e, d=dst_tm, g=g, half=half, eb=eb: e.tensor_tensor(
                                d[:, g, half * 512:(half + 1) * 512],
                                elu_rt[eb][:], elu_m[eb][:], Alu.add),
                            waits=wl_tm, inc=s_vc)
                        elu_free[eb] = t_add
                    if proj == "q":
                        tok_q_done = t_add
                    else:
                        tok_k_done = t_add
                for half in range(2):
                    t_qk = EM.emit(
                        "vector",
                        lambda e, g=g, half=half: e.tensor_tensor(
                            qkf[:], q_tm[:, g, half * 512:(half + 1) * 512],
                            k_tm[:, g, half * 512:(half + 1) * 512], Alu.mult),
                        waits=[tok_q_done, tok_k_done, qk_free], inc=s_vc)
                    t_red = EM.emit(
                        "vector",
                        lambda e, gi=gi, half=half: e.tensor_reduce(
                            invq_tm[:, gi * H + half * 8:gi * H + (half + 1) * 8],
                            qkf[:].rearrange("p (h d) -> p h d", d=DH),
                            AxX, Alu.add),
                        waits=[t_qk], inc=s_vc)
                    qk_free = t_red
                    t_red_last = t_red
                qk_done[g] = (tok_q_done, tok_k_done)
            for g in range(NG):
                gi = t * NG + g
                for cp in range(C):
                    bk = 6 + cp // 4
                    t_S_last = mm(
                        pb[bk][:, (cp % 4) * 128:(cp % 4 + 1) * 128],
                        k_tm[:, g, cp * 128:(cp + 1) * 128],
                        q_tm[:, g, cp * 128:(cp + 1) * 128],
                        start=(gi == 0 and cp % 4 == 0),
                        stop=(gi == H - 1 and cp % 4 == 3),
                        waits=list(qk_done[g]) if cp == 0 else (),
                        bank=bk if (gi == 0 and cp % 4 == 0) else None,
                        inc=(s_tn if cp == C - 1 else None))
            S_tile_tok = t_S_last
        emit_pieces(len(PIECES))

        t_cwv = None
        for g in range(H):
            bk = 4 + (g % 2)
            t_tr = EM.emit(
                "tensor",
                lambda e, g=g, bk=bk: e.transpose(
                    pb[bk][0:8, 0:128], cw_tm[:, g, :], ident_s[:]),
                waits=[t_rtr, t_ident, bank_guard[bk]], inc=s_tn)
            t_cwv = EM.emit(
                "vector",
                lambda e, g=g, bk=bk: e.tensor_copy(
                    cw_fm[:, g * 128:(g + 1) * 128], pb[bk][0:8, 0:128]),
                waits=[t_tr] + ([elu_free[0], elu_free[1], t_red_last] if g == 0 else []),
                inc=s_vc)
            read_bank(bk, t_cwv)
        t_cwr = EM.emit("scalar", lambda e: e.copy(cw_r[:], cw_fm[:]),
                        waits=[t_cwv], inc=s_sc)

        t_invq = vchain("vector", s_vc, [
            lambda e: e.tensor_scalar(invq_tm[:], invq_tm[:], EPS_QK, None, Alu.add),
            lambda e: e.reciprocal(invq_tm[:], invq_tm[:]),
        ], waits=[t_red_last])
        t_iqv = None
        for gi in range(H):
            bk = 4 + gi % 2
            t_tr = EM.emit(
                "tensor",
                lambda e, gi=gi, bk=bk: e.transpose(
                    pb[bk][0:H, 0:128], invq_tm[:, gi * H:(gi + 1) * H], ident_s[:]),
                waits=[t_invq, t_ident, bank_guard[bk]], inc=s_tn)
            t_iqv = EM.emit(
                "vector",
                lambda e, gi=gi, bk=bk: e.tensor_copy(
                    invq_fm[:, gi * 128:(gi + 1) * 128], pb[bk][0:H, 0:128]),
                waits=[t_tr] + ([t_cwr] if gi == 0 else []), inc=s_vc)
            read_bank(bk, t_iqv)
        t_iqr = EM.emit("scalar", lambda e: e.copy(invq_r[:], invq_fm[:]),
                        waits=[t_iqv, last_s2_mm], inc=s_sc)

        def s_extract(e):
            r = None
            for cp in range(C):
                bk = 6 + cp // 4
                cb = (cp % 4) * 128
                e.tensor_copy(S_pk[0:64, cp * 64:(cp + 1) * 64],
                              pb[bk][0:64, cb:cb + 64])
                r = e.tensor_copy(S_pk[64:128, cp * 64:(cp + 1) * 64],
                                  pb[bk][64:128, cb + 64:cb + 128])
            return r
        t_sx = EM.emit("vector", s_extract, waits=[t_S_last], inc=s_vc)
        read_bank(6, t_sx)
        read_bank(7, t_sx)
        t_sst = dma("scalar", s_scc, cc_in, S_pk[:], waits=[t_sx])
        t_cc = EM.emit(
            "gpsimd",
            lambda e: e.collective_compute(
                "AllReduce", Alu.add,
                replica_groups=[list(g) for g in cc_groups],
                ins=[cc_in], outs=[cc_out]),
            waits=[t_sst], inc=s_cc)
        t_sfl = dma("gpsimd", s_gd, S_fl[:], cc_out, waits=[t_cc])

        handover_D = [last_s2_mm, t_S_last, t_red_last, t_iqr, elu_free[0],
                      elu_free[1], cvt_free[0], cvt_free[1], stg_free[0],
                      stg_free[1]]
        fstr_free = [None] * 8
        frfs_free = [None, None]
        cwb_free = [None, None]
        r2s_free = None
        g_written = []
        last_ffn_mm = None
        last_ffn_v = None

        for t in range(NT):
            tsl = slice(t * TT, (t + 1) * TT)
            r2s_tok = None
            u_tok = None
            for e_i in range(E):
                cb = 4 + e_i % 2
                t_cwb = mm(pb[cb][:],
                           hmask_r[0:8, D + e_i * 128:D + (e_i + 1) * 128],
                           cw_r[:, tsl], True, True,
                           waits=[t_cwr, t_hmr, cwb_free[e_i % 2], bank_guard[cb]],
                           inc=s_tn)
                for hf in range(2):
                    sbuf_i = (e_i * 2 + hf) % 2
                    slab = N_UP0 + e_i * 2 + hf
                    wl = [cache_tok[slab], fstr_free[sbuf_i]]
                    if t == 0 and e_i == 0 and hf == 0:
                        wl += handover_D
                    t_w = dma("sync", s_fs[sbuf_i], fstr[sbuf_i][:], c_w[slab], waits=wl)
                    for ic2 in range(2):
                        icg = hf * 2 + ic2
                        bk = icg % 4
                        for kc in range(C):
                            mm(pb[bk][:],
                               fstr[sbuf_i][:, kc * 256 + ic2 * 128:
                                            kc * 256 + (ic2 + 1) * 128],
                               xn_bf[:, kc, tsl],
                               start=(kc == 0), stop=(kc == C - 1),
                               waits=[t_w, xn_done[t]] if kc == 0 else (),
                               bank=bk if kc == 0 else None,
                               inc=(s_tn if kc == C - 1 else None))
                        tok_up = (s_tn, EM.cnt["s_tn"])
                        pp = icg % 2
                        t_fr = EM.emit(
                            "scalar",
                            lambda e, bk=bk, pp=pp: e.activation(
                                fr_t[pp][:], pb[bk][:], Act.Relu),
                            waits=[tok_up, frfs_free[pp],
                                   (handover_D[0] if (t == 0 and e_i == 0) else None)],
                            inc=s_sc)
                        read_bank(bk, t_fr)
                        t_fs = EM.emit(
                            "scalar",
                            lambda e, pp=pp: e.activation(
                                fs_t[pp][:], fr_t[pp][:], Act.Square),
                            waits=[t_fr], inc=s_sc)
                        slot = e_i * 4 + icg
                        wl2 = [t_fs, t_cwb]
                        if t == 0 and slot == 0:
                            wl2 += handover_D
                        if r2s_free is not None and slot == 0:
                            wl2.append(r2s_free)
                        r2s_tok = EM.emit(
                            "vector",
                            lambda e, slot=slot, cb=cb, pp=pp: e.tensor_tensor(
                                r2s[:, slot, :], fs_t[pp][:], pb[cb][:], Alu.mult),
                            waits=wl2, inc=s_vc)
                        frfs_free[pp] = r2s_tok
                        cwb_free[e_i % 2] = r2s_tok
                        read_bank(cb, r2s_tok)
                    fstr_free[sbuf_i] = tok_up
            for j in range(SIC):
                sbuf_i = 2 + (j // 2) % 2
                if j % 2 == 0:
                    slab = N_SU0 + j // 2
                    wl = [cache_tok[slab], fstr_free[sbuf_i]]
                    if t == 0 and j == 0:
                        wl += handover_D
                    t_su = dma("sync", s_fs[sbuf_i], fstr[sbuf_i][:], c_w[slab], waits=wl)
                bk = j % 4
                for kc in range(C):
                    mm(pb[bk][:],
                       fstr[sbuf_i][:, (j % 2) * 1024 + kc * 128:
                                    (j % 2) * 1024 + (kc + 1) * 128],
                       xn_bf[:, kc, tsl],
                       start=(kc == 0), stop=(kc == C - 1),
                       waits=[t_su, xn_done[t]] if kc == 0 else (),
                       bank=bk if kc == 0 else None,
                       inc=(s_tn if kc == C - 1 else None))
                tok_su = (s_tn, EM.cnt["s_tn"])
                pp = j % 2
                t_fr = EM.emit(
                    "scalar",
                    lambda e, bk=bk, pp=pp: e.activation(
                        fr_t[pp][:], pb[bk][:], Act.Relu),
                    waits=[tok_su, frfs_free[pp]], inc=s_sc)
                read_bank(bk, t_fr)
                wl2 = [t_fr]
                if r2s_free is not None and t > 0 and j == 0:
                    wl2.append(r2s_free)
                u_tok = EM.emit(
                    "vector",
                    lambda e, j=j, pp=pp: e.tensor_tensor(
                        u_bf[:, j, :], fr_t[pp][:], fr_t[pp][:], Alu.mult),
                    waits=wl2, inc=s_vc)
                frfs_free[pp] = u_tok
                if j % 2 == 1:
                    fstr_free[sbuf_i] = tok_su
            t_sd_prev = {}
            for j in range(C):
                sd_i = 4 + j % 2
                slab_sd = N_SD0 + j
                wl = [cache_tok[slab_sd], fstr_free[sd_i]]
                t_sd = dma("sync", s_fs[sd_i], fstr[sd_i][:], c_w[slab_sd], waits=wl)
                bk = 4 + j % 2
                first = True
                for sic in range(SIC):
                    mm(pb[bk][:],
                       fstr[sd_i][:, sic * 128:(sic + 1) * 128],
                       u_bf[:, sic, :],
                       start=first, stop=False,
                       waits=[t_sd, u_tok] if first else (),
                       bank=bk if first else None)
                    first = False
                for eh in range(2):
                    dn_i = 6 + eh
                    slab_dn = N_DN0 + j * 2 + eh
                    wl = [cache_tok[slab_dn], fstr_free[dn_i]]
                    t_dn = dma("sync", s_fs[dn_i], fstr[dn_i][:], c_w[slab_dn], waits=wl)
                    for e4 in range(4):
                        for ic in range(4):
                            is_last = (eh == 1 and e4 == 3 and ic == 3)
                            is_last_eh = (e4 == 3 and ic == 3)
                            last_ffn_mm = mm(
                                pb[bk][:],
                                fstr[dn_i][:, e4 * 512 + ic * 128:
                                           e4 * 512 + (ic + 1) * 128],
                                r2s[:, (eh * 4 + e4) * 4 + ic, :],
                                start=False, stop=is_last,
                                waits=[t_dn, r2s_tok] if (e4 == 0 and ic == 0) else (),
                                inc=(s_tn if is_last_eh else None))
                    fstr_free[dn_i] = last_ffn_mm
                fstr_free[sd_i] = last_ffn_mm
                wl_g = [last_ffn_mm]
                if t == 0 and j == 0:
                    wl_g += [cvt_free[0], cvt_free[1], t_iqr]
                t_gw = EM.emit(
                    "scalar",
                    lambda e, j=j, bk=bk, tsl=tsl: e.copy(
                        g_bf[:, j, tsl], pb[bk][:]),
                    waits=wl_g, inc=s_sc)
                read_bank(bk, t_gw)
                g_written.append(t_gw)
            r2s_free = last_ffn_mm

        bds_fns = [lambda e: e.memset(BDS[:], 0.0)]
        for cp in range(C):
            bds_fns.append(lambda e, cp=cp: e.tensor_copy(
                BDS[0:64, cp, 0:64], S_fl[0:64, cp * 64:(cp + 1) * 64]))
            bds_fns.append(lambda e, cp=cp: e.tensor_copy(
                BDS[64:128, cp, 64:128], S_fl[64:128, cp * 64:(cp + 1) * 64]))
        t_bds = vchain("vector", s_vc, bds_fns, waits=[t_sfl])

        wst_free = [None, None]
        ysq_free = [None, None]
        xE_free = [None, None]
        v1_guard = r2s_free
        vpp_free = [None, None]
        t_out_last = None

        def emit_combine(ctx, step):
            if step == 0:
                ctx["t_A2"] = mm(pb[2][:], ones128_r[:], lnr_r[0:1, 0, :],
                                 True, True,
                                 waits=[ctx["t_lncp2"], bank_guard[2]], inc=s_tn)
                ctx["t_B2"] = mm(pb[3][:], ones128_r[:], lnr_r[0:1, 1, :],
                                 True, True, waits=[bank_guard[3]], inc=s_tn)
                ctx["last_lnr"] = ctx["t_B2"]
            mo = step
            t_sil = EM.emit(
                "scalar",
                lambda e, mo=mo, tsl=ctx["tsl"]: e.activation(
                    s_gs[:], g_bf[:, mo, tsl], Act.Silu),
                waits=[g_written[-1], ctx.get("t_cmb")], inc=s_sc)

            def s_chain(e, mo=mo, par=ctx["par"]):
                e.tensor_tensor(s_t1[:], y_t[:, mo, :], pb[2][:], Alu.mult)
                e.tensor_tensor(s_t1[:], s_t1[:], pb[3][:], Alu.add)
                e.tensor_scalar(s_t1[:], s_t1[:], bcols_s[:, 5, mo:mo + 1],
                                bcols_s[:, 6, mo:mo + 1], Alu.mult, Alu.add)
                e.tensor_tensor(s_t1[:], s_t1[:], s_gs[:], Alu.mult)
                return e.tensor_tensor(
                    xtile[par][:, mo, :], xtile[par][:, mo, :], s_t1[:], Alu.add)
            ctx["t_cmb"] = EM.emit(
                "vector", s_chain,
                waits=[ctx["t_A2"], ctx["t_B2"], t_sil, ctx["t_x2"], t_bcl],
                inc=s_vc)
            if step == C - 1:
                read_bank(2, ctx["t_cmb"])
                read_bank(3, ctx["t_cmb"])
                t_out = dma("scalar", s_ob[ctx["par"]],
                            out_d[:, :, ctx["tsl"]], xtile[ctx["par"]][:],
                            waits=[ctx["t_cmb"]])
                xE_free[ctx["par"]] = t_out
                ctx["t_out"] = t_out

        pend = None
        last_ctx = None
        for t in range(NT):
            par = t % 2
            tsl = slice(t * TT, (t + 1) * TT)
            t_v1 = None
            for mo in range(C):
                wi = (mo // 2) % 2
                if mo % 2 == 0:
                    slab = N_WV0 + mo // 2
                    wl = [cache_tok[slab], wst_free[wi]]
                    if t == 0 and mo == 0:
                        wl += [v1_guard, frfs_free[0], frfs_free[1]]
                    t_wv = dma("sync", s_ws[wi], wst[wi][:], c_w[slab], waits=wl)
                    if mo == 2:
                        wlx = [xE_free[par]]
                        if t < 2:
                            wlx += [last_ffn_mm]
                        t_x2 = dma("sync", s_x[par], xtile[par][:],
                                   di["x"][:, :, tsl], waits=wlx)
                bk = (0, 1, 4, 5)[mo % 4]
                for kc in range(C):
                    mm(pb[bk][:],
                       wst[wi][:, kc * 256 + (mo % 2) * 128:
                               kc * 256 + (mo % 2 + 1) * 128],
                       xn_bf[:, kc, tsl],
                       start=(kc == 0), stop=(kc == C - 1),
                       waits=[t_wv, xn_done[t]] if kc == 0 else (),
                       bank=bk if kc == 0 else None,
                       inc=(s_tn if kc == C - 1 else None))
                tok_b = (s_tn, EM.cnt["s_tn"])
                wl2 = [tok_b, t_bcl]
                if t == 0 and mo == 0:
                    wl2.append(v1_guard)
                if mo % 2 == 0:
                    t_v1 = EM.emit(
                        "scalar",
                        lambda e, mo=mo, bk=bk: e.activation(
                            v1_bf[:, mo, :], pb[bk][:], Act.Identity,
                            bias=bcols_s[:, 2, mo:mo + 1], scale=1.0),
                        waits=wl2, inc=s_sc)
                else:
                    t_v1 = EM.emit(
                        "vector",
                        lambda e, mo=mo, bk=bk: e.tensor_scalar(
                            v1_bf[:, mo, :], pb[bk][:],
                            bcols_s[:, 2, mo:mo + 1], None, Alu.add),
                        waits=wl2, inc=s_vc)
                read_bank(bk, t_v1)
                if mo % 2 == 1:
                    wst_free[wi] = tok_b
                if pend is not None:
                    emit_combine(pend, mo)
            if pend is not None:
                last_lnr_rd = pend["last_lnr"]
                last_ctx = pend
                pend = None
            t_mix = None
            for cch in range(C):
                wi = (cch // 2) % 2
                if cch % 2 == 0:
                    slab = N_WV20 + cch // 2
                    t_wv2 = dma("sync", s_ws[wi], wst[wi][:], c_w[slab],
                                waits=[cache_tok[slab], wst_free[wi]])
                bk = (0, 1, 4, 5)[cch % 4]
                for kc in range(C):
                    mm(pb[bk][:],
                       wst[wi][:, kc * 256 + (cch % 2) * 128:
                               kc * 256 + (cch % 2 + 1) * 128],
                       v1_bf[:, kc, :],
                       start=(kc == 0), stop=(kc == C - 1),
                       waits=[t_wv2, t_v1] if kc == 0 else (),
                       bank=bk if kc == 0 else None,
                       inc=(s_tn if kc == C - 1 else None))
                tok_b = (s_tn, EM.cnt["s_tn"])
                bx = 2 + cch % 2
                tok_x = mm(pb[bx][:], hmask_r[:, cch * 128:(cch + 1) * 128],
                           invq_r[:, tsl], True, True,
                           waits=[t_iqr, t_hmr, bank_guard[bx]], inc=s_tn)
                pp = cch % 2
                wl_qx = [tok_x, vpp_free[pp]]
                if t == 0 and cch < 2:
                    wl_qx.append(v1_guard)
                t_qx = EM.emit(
                    "scalar",
                    lambda e, pp=pp, bx=bx: e.copy(iqx[pp][:], pb[bx][:]),
                    waits=wl_qx, inc=s_sc)
                read_bank(bx, t_qx)
                t_vpp = EM.emit(
                    "vector",
                    lambda e, pp=pp, bk=bk, cch=cch: e.scalar_tensor_tensor(
                        v_pp[pp][:], pb[bk][:], bcols_s[:, 3, cch:cch + 1],
                        iqx[pp][:], Alu.add, Alu.mult),
                    waits=[tok_b, t_qx, t_bcl, vpp_free[pp]], inc=s_vc)
                read_bank(bk, t_vpp)
                bs = 6 + cch % 2
                tok_vs = mm(pb[bs][:], BDS[:, cch, :], v_pp[pp][:], True, True,
                            waits=[t_bds, t_vpp, bank_guard[bs]], inc=s_tn)
                vpp_free[pp] = tok_vs
                t_mix = EM.emit(
                    "scalar",
                    lambda e, cch=cch, bs=bs: e.copy(mix_bf[:, cch, :], pb[bs][:]),
                    waits=[tok_vs], inc=s_sc)
                read_bank(bs, t_mix)
                if cch % 2 == 1:
                    wst_free[wi] = tok_b
            t_y = None
            tok_ss = None
            for mo in range(C):
                wi = (mo // 2) % 2
                if mo % 2 == 0:
                    slab = N_WO0 + mo // 2
                    t_wo = dma("sync", s_ws[wi], wst[wi][:], c_w[slab],
                               waits=[cache_tok[slab], wst_free[wi]])
                bk = (0, 1, 2, 3)[mo % 4]
                for kc in range(C):
                    mm(pb[bk][:],
                       wst[wi][:, kc * 256 + (mo % 2) * 128:
                               kc * 256 + (mo % 2 + 1) * 128],
                       mix_bf[:, kc, :],
                       start=(kc == 0), stop=(kc == C - 1),
                       waits=[t_wo, t_mix] if kc == 0 else (),
                       bank=bk if kc == 0 else None,
                       inc=(s_tn if kc == C - 1 else None))
                tok_b = (s_tn, EM.cnt["s_tn"])
                wl2 = [tok_b, t_bcl]
                if t == 0 and mo == 0:
                    wl2 += [v1_guard]
                t_y = EM.emit(
                    "vector",
                    lambda e, mo=mo, bk=bk, tsl=tsl: e.scalar_tensor_tensor(
                        y_t[:, mo, :], pb[bk][:], bcols_s[:, 4, mo:mo + 1],
                        xn_bf[:, mo, tsl], Alu.add, Alu.add),
                    waits=wl2, inc=s_vc)
                read_bank(bk, t_y)
                if mo % 2 == 1:
                    wst_free[wi] = tok_b
                mm(pb[4][0:1, :], cones_r[:], y_t[:, mo, :],
                   start=(mo == 0), stop=(mo == C - 1),
                   waits=[t_y], bank=4 if mo == 0 else None)
                pp = mo % 2
                t_ysq = EM.emit(
                    "scalar",
                    lambda e, mo=mo, pp=pp: e.activation(
                        ysq[pp][:], y_t[:, mo, :], Act.Square),
                    waits=[t_y, ysq_free[pp]], inc=s_sc)
                tok_ss = mm(pb[5][0:1, :], cones_r[:], ysq[pp][:],
                            start=(mo == 0), stop=(mo == C - 1), waits=[t_ysq],
                            bank=5 if mo == 0 else None, inc=s_tn)
                ysq_free[pp] = tok_ss
            t_row2 = vchain("vector", s_vc, [
                lambda e: e.tensor_scalar(lnf[0:1, 0, :], pb[4][0:1, :], 1.0 / D, None, Alu.mult),
                lambda e: e.tensor_tensor(lnf[0:1, 1, :], lnf[0:1, 0, :], lnf[0:1, 0, :], Alu.mult),
                lambda e: e.scalar_tensor_tensor(lnf[0:1, 1, :], pb[5][0:1, :], 1.0 / D,
                                                 lnf[0:1, 1, :], Alu.mult, Alu.subtract),
                lambda e: e.tensor_scalar(lnf[0:1, 1, :], lnf[0:1, 1, :], EPS_LN2, None, Alu.add),
            ], waits=[tok_ss, last_lnr_rd])
            read_bank(4, t_row2)
            read_bank(5, t_row2)
            t_ln2 = EM.emit(
                "scalar", lambda e: e.activation(lnf[0:1, 1, :], lnf[0:1, 1, :], Act.Ln),
                waits=[t_row2], inc=s_sc)
            t_rstd2 = EM.emit(
                "scalar", lambda e: e.activation(lnr_r[0:1, 0, :], lnf[0:1, 1, :],
                                                 Act.Exp, scale=-0.5),
                waits=[t_ln2], inc=s_sc)
            t_lncp2 = EM.emit(
                "vector",
                lambda e: e.scalar_tensor_tensor(
                    lnr_r[0:1, 1, :], lnf[0:1, 0, :], -1.0, lnr_r[0:1, 0, :],
                    Alu.mult, Alu.mult),
                waits=[t_rstd2], inc=s_vc)
            pend = {"par": par, "tsl": tsl, "t_lncp2": t_lncp2, "t_x2": t_x2}
        for mo in range(C):
            emit_combine(pend, mo)
        last_lnr_rd = pend["last_lnr"]
        t_out_last = pend["t_out"]

        if debug:
            t_d = dma("gpsimd", s_gd, dbg["d_sct"], SCt[:], waits=[t_rtr])
            t_d = dma("gpsimd", s_gd, dbg["d_sfc"], sfc[:], waits=[t_rtr])
            t_d = dma("gpsimd", s_gd, dbg["d_selw"], selw[:], waits=[t_rtr])
            t_d = dma("gpsimd", s_gd, dbg["d_cwtm"], cw_tm[:], waits=[t_rtr])
            t_d = dma("gpsimd", s_gd, dbg["d_spk"], S_pk[:], waits=[t_sx])
            t_d = dma("gpsimd", s_gd, dbg["d_sfl"], S_fl[:], waits=[t_sfl])
            t_d = dma("gpsimd", s_gd, dbg["d_xn"], xn_bf[:], waits=[xn_done[-1]])
            t_d = dma("gpsimd", s_gd, dbg["d_g"], g_bf[:], waits=[g_written[-1]])
            t_d = dma("gpsimd", s_gd, dbg["d_iq"], invq_tm[:], waits=[t_invq])
            t_d = dma("gpsimd", s_gd, dbg["d_cw"], cw_r[:].bitcast(F32), waits=[t_cwr])
            t_d = dma("gpsimd", s_gd, dbg["d_v1"], v1_bf[:], waits=[t_v1])
            t_d = dma("gpsimd", s_gd, dbg["d_mix"], mix_bf[:], waits=[t_mix])
            EM.wait("gpsimd", t_d)
        EM.wait("scalar", (s_ob[0], EM.cnt.get("s_ob0", 0)),
                (s_ob[1], EM.cnt.get("s_ob1", 0)))

        block = es.enter_context(nc.Block())

        @block.sync
        def _(sync):
            for fn in EM.prog["sync"]:
                fn(sync)

        @block.scalar
        def _(scalar):
            for fn in EM.prog["scalar"]:
                fn(scalar)

        @block.vector
        def _(vector):
            for fn in EM.prog["vector"]:
                fn(vector)

        @block.tensor
        def _(tensor):
            for fn in EM.prog["tensor"]:
                fn(tensor)

        @block.gpsimd
        def _(gpsimd):
            for fn in EM.prog["gpsimd"]:
                fn(gpsimd)

    return nc


_CACHE = {}


def _pack_fm(w):
    return np.ascontiguousarray(w.reshape(C, 128, -1).transpose(1, 0, 2))


def _slabs_dout(w):
    a = w.reshape(C, 128, 1024).transpose(1, 0, 2)
    return [np.ascontiguousarray(a[:, :, q * 256:(q + 1) * 256].reshape(128, 2048))
            for q in range(4)]


def _build_inputs(inp):
    x = np.asarray(inp["L0_attn_x"], dtype=np.float32)
    B, L, _ = x.shape
    assert (B, L) == (2, 8192)

    w1 = np.empty((16, 128, 2048), np.float32)
    for wi, nm in enumerate(("Wq", "Wk", "Wq2", "Wk2")):
        a = np.asarray(inp[f"L0_attn_{nm}"], np.float32)
        flat = a.reshape(C, 128, 1024).transpose(1, 0, 2).reshape(128, 8192)
        for q in range(4):
            w1[wi * 4 + q] = flat[:, q * 2048:(q + 1) * 2048]

    wffn = np.empty((N_WSLAB, 128, 2048), np.float32)
    up = np.asarray(inp["L0_ffn_up_weights"], np.float32)
    dn = np.asarray(inp["L0_ffn_down_weights"], np.float32)
    su = np.asarray(inp["L0_ffn_shared_up_weight"], np.float32)
    sd = np.asarray(inp["L0_ffn_shared_down_weight"], np.float32)
    for e_i in range(E):
        upT = up[e_i].T
        a = upT.reshape(C, 128, 512).transpose(1, 0, 2)
        for hf in range(2):
            wffn[N_UP0 + e_i * 2 + hf] = a[:, :, hf * 256:(hf + 1) * 256].reshape(128, 2048)
    for j in range(C):
        for eh in range(2):
            cols = []
            for e4 in range(4):
                dnT = dn[eh * 4 + e4].T
                blk = dnT.reshape(4, 128, 1024)[:, :, j * 128:(j + 1) * 128]
                cols.append(blk.transpose(1, 0, 2).reshape(128, 512))
            wffn[N_DN0 + j * 2 + eh] = np.concatenate(cols, axis=1)
    suT = su.T
    for j2 in range(E):
        cols = []
        for j in (2 * j2, 2 * j2 + 1):
            blk = suT.reshape(C, 128, 2048)[:, :, j * 128:(j + 1) * 128]
            cols.append(blk.transpose(1, 0, 2).reshape(128, 1024))
        wffn[N_SU0 + j2] = np.concatenate(cols, axis=1)
    sdT = sd.T
    for j in range(C):
        blk = sdT.reshape(SIC, 128, 1024)[:, :, j * 128:(j + 1) * 128]
        wffn[N_SD0 + j] = blk.transpose(1, 0, 2).reshape(128, 2048)
    for base, nm in ((N_WV0, "Wv"), (N_WV20, "Wv2"), (N_WO0, "Wo")):
        for q, slab in enumerate(_slabs_dout(np.asarray(inp[f"L0_attn_{nm}"], np.float32))):
            wffn[base + q] = slab

    brows = np.zeros((1, 7, 512), np.float32)
    brows[0, 3] = np.asarray(inp["L0_attn_bq2"], np.float32)[0:512]
    brows[0, 4] = np.asarray(inp["L0_attn_bq2"], np.float32)[512:1024]
    brows[0, 5] = np.asarray(inp["L0_attn_bk2"], np.float32)[0:512]
    brows[0, 6] = np.asarray(inp["L0_attn_bk2"], np.float32)[512:1024]
    bcols = np.zeros((128, 7, C), np.float32)
    for bi, nm in enumerate(("bq", "bk", "bv", "bv2", "bo", "gamma", "beta")):
        bcols[:, bi, :] = np.asarray(
            inp[f"L0_attn_{nm}"], np.float32).reshape(C, 128).T
    gatet = _pack_fm(np.ascontiguousarray(
        np.asarray(inp["L0_ffn_gate_weight"], np.float32).T))
    gbias = np.ascontiguousarray(np.broadcast_to(
        np.asarray(inp["L0_ffn_gate_bias"], np.float32), (128, H, E)))
    hmask = np.zeros((H, 2 * D), np.float32)
    for h in range(H):
        hmask[h, h * DH:(h + 1) * DH] = 1.0
    for e_i in range(E):
        hmask[e_i, D + e_i * 128:D + (e_i + 1) * 128] = 1.0

    shared = {
        "w1": w1, "wffn": wffn, "brows": brows, "bcols": bcols,
        "gatet": gatet, "gbias": gbias,
        "ident": np.eye(128, dtype=np.float32),
        "hmask": hmask,
    }
    x_fm = x.transpose(0, 2, 1)
    in_maps = []
    for core in range(8):
        b = core // 4
        sl = x_fm[b, :, (core % 4) * TC:(core % 4 + 1) * TC]
        xc = np.ascontiguousarray(sl.reshape(C, 128, TC).transpose(1, 0, 2))
        m = dict(shared)
        m["x"] = xc
        in_maps.append(m)
    return in_maps


def kernel(**inputs):
    if "nc" not in _CACHE:
        _CACHE["nc"] = build_program()
    nc = _CACHE["nc"]
    in_maps = _build_inputs(inputs)
    res = run_bass_kernel_spmd(nc, in_maps, core_ids=list(range(8)))
    out = np.empty((2, 8192, D), np.float32)
    for core in range(8):
        oc = np.asarray(res.results[core]["out"])
        fm = oc.transpose(1, 0, 2).reshape(D, TC)
        b = core // 4
        t0 = (core % 4) * TC
        out[b, t0:t0 + TC, :] = fm.T
    return out


# version 22
# speedup vs baseline: 1.0297x; 1.0043x over previous
import contextlib

import numpy as np

import concourse.bass as bass
import concourse.mybir as mybir
from concourse.bass_utils import run_bass_kernel_spmd

F32 = mybir.dt.float32
F32R = mybir.dt.float32r
BF16 = mybir.dt.bfloat16
Alu = mybir.AluOpType
Act = mybir.ActivationFunctionType
AxX = mybir.AxisListType.X

D = 1024
C = 8
TC = 2048
TT = 512
NT = TC // TT
NG = TT // 128
H = 16
DH = 64
E = 8
SIC = 16
EPS_LN1 = 1e-5
EPS_LN2 = 1e-3
EPS_QK = 1e-6
RSCALE = 2.5

ENGS = ("sync", "scalar", "vector", "tensor", "gpsimd")

SB_TOP = 224 * 1024 - 64
RA = SB_TOP - 64 * 1024
RB = RA - 32 * 1024
RC = RB - 32 * 1024

N_UP0 = 0
N_DN0 = 16
N_SU0 = 32
N_SD0 = 40
N_WV0 = 48
N_WV20 = 52
N_WO0 = 56
N_WSLAB = 60


class _Em:

    def __init__(self):
        self.prog = {e: [] for e in ENGS}
        self.cnt = {}
        self.hwm = {}

    def emit(self, eng, fn, waits=(), inc=None):
        wl = []
        for tok in waits:
            if tok is None:
                continue
            s, v = tok
            key = (eng, s.name)
            if self.hwm.get(key, -1) >= v:
                continue
            self.hwm[key] = v
            wl.append((s, v))
        tok = None
        if inc is not None:
            sem, amt = inc if isinstance(inc, tuple) else (inc, 1)
            self.cnt[sem.name] = self.cnt.get(sem.name, 0) + amt
            tok = (sem, self.cnt[sem.name])
            inc = (sem, amt)

        def run(engine, fn=fn, wl=wl, inc=inc):
            for s, v in wl:
                engine.wait_ge(s, v)
            inst = fn(engine)
            if inc is not None:
                inst.then_inc(inc[0], inc[1])

        self.prog[eng].append(run)
        return tok

    def wait(self, eng, *toks):
        wl = []
        for tok in toks:
            if tok is None:
                continue
            s, v = tok
            key = (eng, s.name)
            if self.hwm.get(key, -1) >= v:
                continue
            self.hwm[key] = v
            wl.append((s, v))
        if wl:
            def run(engine, wl=wl):
                for s, v in wl:
                    engine.wait_ge(s, v)
            self.prog[eng].append(run)


def build_program(cc_groups=((0, 1, 2, 3), (4, 5, 6, 7)), debug=False):
    nc = bass.Bass("TRN2", target_bir_lowering=False,
                   detect_race_conditions=False)

    di = {}
    def din(name, shape):
        di[name] = nc.dram_tensor(name, list(shape), F32, kind="ExternalInput").ap()
    din("x", (128, C, TC))
    din("w1", (16, 128, 2048))
    din("wffn", (N_WSLAB, 128, 2048))
    din("brows", (1, 7, 512))
    din("bcols", (128, 7, C))
    din("gatet", (128, C, E))
    din("gbias", (128, H, E))
    din("ident", (128, 128))
    din("hmask", (H, 2 * D))
    out_d = nc.dram_tensor("out", [128, C, TC], F32, kind="ExternalOutput").ap()
    dbg = {}
    if debug:
        for nm, shp, dt_ in (("d_sct", (128, H, E), F32), ("d_sfc", (128, H, E), F32),
                             ("d_selw", (128, H, E), F32), ("d_cwtm", (128, H, E), F32),
                             ("d_spk", (128, 512), F32), ("d_sfl", (128, 512), F32),
                             ("d_xn", (128, C, TC), BF16), ("d_g", (128, C, TC), BF16),
                             ("d_iq", (128, H * H), F32), ("d_cw", (E, TC), F32),
                             ("d_v1", (128, C, TT), BF16), ("d_mix", (128, C, TT), BF16)):
            dbg[nm] = nc.dram_tensor(nm, list(shp), dt_, kind="ExternalOutput").ap()

    c_w = nc.dram_tensor("c_w", [N_WSLAB, 128, 2048], BF16).ap()
    cc_in = nc.dram_tensor("cc_in", [128, 512], F32).ap()
    cc_out = nc.dram_tensor("cc_out", [128, 512], F32).ap()

    with contextlib.ExitStack() as es:
        def sbn(name, shape, dtype):
            return es.enter_context(nc.sbuf_tensor(name, list(shape), dtype))

        def sba(name, shape, dtype, off):
            return nc.alloc_sbuf_tensor_at(name, list(shape), dtype, offset=off)

        xn_bf = sbn("xn_bf", (128, C, TC), BF16)
        lnr_r = sbn("lnr_r", (1, 7, 512), F32R)
        lnf = sbn("lnf", (1, 2, 512), F32)
        bcols_s = sbn("bcols_s", (128, 7, C), F32)
        SCt = sbn("SCt", (128, H, E), F32)
        sfc = sbn("sfc", (128, H, E), F32)
        msk = sbn("msk", (128, H, E), F32)
        selw = sbn("selw", (128, H, E), F32)
        cw_tm = sbn("cw_tm", (128, H, E), F32)
        rt = [sbn(f"rt{i}", (128, H), F32) for i in range(8)]
        cw_r = sbn("cw_r", (E, TC), F32R)
        invq_tm = sbn("invq_tm", (128, H * H), F32)
        S_pk = sbn("S_pk", (128, 512), F32)
        S_fl = sbn("S_fl", (128, 512), F32)
        BDS = sbn("BDS", (128, C, 128), BF16)
        gatet_s = sbn("gatet_s", (128, C, E), F32)
        gbias_s = sbn("gbias_s", (128, H, E), F32)
        cones_s = sbn("cones_s", (128, 1), F32)
        cones_r = sbn("cones_r", (128, 1), F32R)
        ones128_r = sbn("ones128_r", (1, 128), F32R)
        ident_s = sbn("ident_s", (128, 128), F32)
        hmask_r = sbn("hmask_r", (H, 2 * D), F32R)

        wslot = [sba(f"wslot{i}", (128, C, D), BF16, RA + i * 16384)
                 for i in range(4)]
        r2s = sba("r2s", (128, E * 4, TT), BF16, RA)
        u_bf = sba("u_bf", (128, SIC, TT), BF16, RA + 32768)
        fr_t = [sba(f"fr_t{i}", (128, TT), F32, RA + 49152 + i * 2048)
                for i in range(2)]
        fs_t = [sba(f"fs_t{i}", (128, TT), F32, RA + 53248 + i * 2048)
                for i in range(2)]
        y_t = sba("y_t", (128, C, TT), F32R, RA)
        v1_bf = sba("v1_bf", (128, C, TT), BF16, RA + 16384)
        mix_bf = sba("mix_bf", (128, C, TT), BF16, RA + 24576)
        wst = [sba(f"wst{i}", (128, 2048), BF16, RA + 32768 + i * 4096)
               for i in range(2)]
        ysq = [sba(f"ysq{i}", (128, TT), F32R, RA + 40960 + i * 2048)
               for i in range(2)]
        s_gs = sba("s_gs", (128, TT), F32, RA + 45056)
        s_t1 = sba("s_t1", (128, TT), F32, RA + 47104)
        v_pp = [sba(f"v_pp{i}", (128, TT), BF16, RA + 49152 + i * 1024)
                for i in range(2)]
        invq_r = sba("invq_r", (H, TC), F32R, RA + 57344)
        iqx = [sba(f"iqx{i}", (128, TT), F32, RA + 51200 + i * 2048)
               for i in range(2)]

        xtile = [sba(f"xtile{i}", (128, C, TT), F32, RB + i * 16384)
                 for i in range(2)]
        q1_bf = sba("q1_bf", (128, C, TT), BF16, RB)
        k1_bf = sba("k1_bf", (128, C, TT), BF16, RB + 8192)
        q_tm = sba("q_tm", (128, NG, D), BF16, RB + 16384)
        k_tm = sba("k_tm", (128, NG, D), BF16, RB + 24576)
        fstr = [sba(f"fstr{i}", (128, 2048), BF16, RB + i * 4096)
                for i in range(8)]

        wstg = [sba(f"wstg{i}", (128, 2048), F32, RC + i * 8192)
                for i in range(2)]
        wcvt = [sba(f"wcvt{i}", (128, 2048), BF16, RC + 16384 + i * 4096)
                for i in range(2)]
        xsq = [sba(f"xsq{i}", (128, TT), F32R, RC + 24576 + i * 2048)
               for i in range(2)]
        xnf = [sba(f"xnf{i}", (128, TT), F32, RC + 28672 + i * 2048)
               for i in range(2)]
        cw_fm = sba("cw_fm", (E, TC), F32, RC + 24576)
        invq_fm = sba("invq_fm", (H, TC), F32, RC + 24576)
        g_bf = sba("g_bf", (128, C, TC), BF16, RC)
        brows_s = sba("brows_s", (1, 7, 512), F32, RC)
        hmask_s = sba("hmask_s", (H, 2 * D), F32, RC + 16384)
        elu_m = [sba(f"elu_m{i}", (128, 512), F32, RC + 24576 + i * 2048)
                 for i in range(2)]
        elu_rt = [sba(f"elu_rt{i}", (128, 512), F32, RC + 28672 + i * 2048)
                  for i in range(2)]
        qkf = sbn("qkf", (128, 512), BF16)

        pb = [es.enter_context(nc.psum_tensor(f"pb{i}", [128, 512], F32))
              for i in range(8)]

        s_in = es.enter_context(nc.semaphore("s_in"))
        s_x = [es.enter_context(nc.semaphore(f"s_x{i}")) for i in range(2)]
        s_wp = [es.enter_context(nc.semaphore(f"s_wp{i}")) for i in range(2)]
        s_cw = [es.enter_context(nc.semaphore(f"s_cw{i}")) for i in range(2)]
        s_fs = [es.enter_context(nc.semaphore(f"s_fs{i}")) for i in range(8)]
        s_ws = [es.enter_context(nc.semaphore(f"s_ws{i}")) for i in range(2)]
        s_ob = [es.enter_context(nc.semaphore(f"s_ob{i}")) for i in range(2)]
        s_scc = es.enter_context(nc.semaphore("s_scc"))
        s_gd = es.enter_context(nc.semaphore("s_gd"))
        s_gd2 = es.enter_context(nc.semaphore("s_gd2"))
        s_tn = es.enter_context(nc.semaphore("s_tn"))
        s_vc = es.enter_context(nc.semaphore("s_vc"))
        s_sc = es.enter_context(nc.semaphore("s_sc"))
        s_cc = es.enter_context(nc.semaphore("s_cc"))

        assert nc.sbuf_bytes_remaining >= SB_TOP - RC, (
            "P0 allocations overflow into overlay regions: remaining="
            f"{nc.sbuf_bytes_remaining}, need >= {SB_TOP - RC}")

        EM = _Em()
        bank_guard = [None] * 8

        def dma(q, sem, dst, src, waits=()):
            prev = EM.cnt.get(sem.name, 0)
            w = list(waits) + ([(sem, prev)] if prev else [])
            return EM.emit(q, lambda e, d=dst, s=src: e.dma_start(out=d, in_=s),
                           waits=w, inc=(sem, 16))

        def mm(out, lhsT, rhs, start, stop, waits=(), inc=None, bank=None):
            w = list(waits)
            if start and bank is not None and bank_guard[bank] is not None:
                w.append(bank_guard[bank])
            return EM.emit(
                "tensor",
                lambda e, o=out, l=lhsT, r=rhs, st=start, sp=stop:
                    e.matmul(o, l, r, start=st, stop=sp),
                waits=w, inc=inc)

        def read_bank(bank, tok):
            bank_guard[bank] = tok

        def vchain(eng, sem, fns, waits=()):
            tok = None
            for i, fn in enumerate(fns):
                w = list(waits) if i == 0 else [tok]
                tok = EM.emit(eng, fn, waits=w, inc=sem)
            return tok

        t_ident = dma("sync", s_in, ident_s[:], di["ident"])
        t_avl = dma("sync", s_in, brows_s[:], di["brows"])
        t_hml = dma("sync", s_in, hmask_s[:], di["hmask"])
        t_gatet = dma("sync", s_in, gatet_s[:], di["gatet"])
        t_gbias = dma("sync", s_in, gbias_s[:], di["gbias"])
        t_bcl = dma("sync", s_in, bcols_s[:], di["bcols"])
        t_avr = EM.emit("scalar", lambda e: e.copy(lnr_r[0:1, 0:7, :], brows_s[:]),
                        waits=[t_avl], inc=s_sc)
        t_hmr = EM.emit("scalar", lambda e: e.copy(hmask_r[:], hmask_s[:]),
                        waits=[t_hml], inc=s_sc)

        def init_consts(e):
            e.memset(cones_s[:], 1.0)
            return e.memset(lnf[0:1, 0, :], 1.0)
        t_onesf = EM.emit("vector", init_consts, waits=[], inc=s_vc)

        def init_consts2(e):
            e.copy(ones128_r[:], lnf[0:1, 0, 0:128])
            e.copy(cones_r[:], cones_s[:])
            return e.copy(lnr_r[0:1, 2, :], lnf[0:1, 0, :])
        t_ones = EM.emit("scalar", init_consts2, waits=[t_onesf, t_avr], inc=s_sc)
        t_cones = t_ones
        t_o128 = t_ones

        stg_free = [None, None]
        cvt_free = [None, None]
        piece_i = [0]
        cache_tok = {}

        def w_piece(src_ap, dst_sbuf=None, cache_slab=None):
            i = piece_i[0] % 2
            ceng = (("scalar", s_sc), ("vector", s_vc))[piece_i[0] % 2]
            piece_i[0] += 1
            w0 = [t_avr, t_hmr] if piece_i[0] <= 2 else []
            t_l = dma("sync", s_wp[i], wstg[i][:], src_ap,
                      waits=[stg_free[i]] + w0)
            use_tc = ceng[0] == "vector"
            if dst_sbuf is not None:
                t_c = EM.emit(ceng[0],
                              lambda e, i=i, d=dst_sbuf, u=use_tc: e.tensor_copy(d, wstg[i][:])
                              if u else e.copy(d, wstg[i][:]),
                              waits=[t_l], inc=ceng[1])
                stg_free[i] = t_c
                return t_c
            wcv = [t_hmr] if piece_i[0] <= 18 else []
            t_c = EM.emit(ceng[0],
                          lambda e, i=i, u=use_tc: e.tensor_copy(wcvt[i][:], wstg[i][:])
                          if u else e.copy(wcvt[i][:], wstg[i][:]),
                          waits=[t_l, cvt_free[i]] + wcv, inc=ceng[1])
            stg_free[i] = t_c
            t_s = dma("scalar", s_cw[i], c_w[cache_slab], wcvt[i][:], waits=[t_c])
            cvt_free[i] = t_s
            cache_tok[cache_slab] = t_s
            return t_s

        w1_tok = {}
        PIECES = []
        for wi in range(4):
            for q in range(4):
                dst = wslot[wi][:, 2 * q:2 * q + 2, :].rearrange("p a b -> p (a b)")
                PIECES.append(("w1", wi, di["w1"][4 * wi + q], dst, None))
        for s in range(N_WSLAB):
            PIECES.append(("wffn", None, di["wffn"][s], None, s))
        piece_cursor = [0]

        def emit_pieces(n):
            while n > 0 and piece_cursor[0] < len(PIECES):
                kind, wi, src_ap, dst, slab = PIECES[piece_cursor[0]]
                piece_cursor[0] += 1
                tok = w_piece(src_ap, dst_sbuf=dst, cache_slab=slab)
                if kind == "w1":
                    w1_tok[wi] = tok
                n -= 1


        x_free = [(), ()]
        xn_done = []
        t_sig = None
        xsq_free = [None, None]
        xnf_free = [None, None]
        xnf_sc_free = [None, None]
        last_lnr_rd = None

        x_tok = [None] * NT
        x_tok[0] = dma("sync", s_x[0], xtile[0][:], di["x"][:, :, 0:TT])
        for t in range(NT):
            par = t % 2
            tsl = slice(t * TT, (t + 1) * TT)
            if t + 1 < NT:
                x_tok[t + 1] = dma("sync", s_x[(t + 1) % 2], xtile[(t + 1) % 2][:],
                                   di["x"][:, :, (t + 1) * TT:(t + 2) * TT],
                                   waits=list(x_free[(t + 1) % 2]))
            t_x = x_tok[t]
            t_sum = None
            for c in range(C):
                t_sum = mm(pb[0][0:1, :], cones_s[:], xtile[par][:, c, :],
                           start=(c == 0), stop=(c == C - 1),
                           waits=[t_x, t_cones] if c == 0 else (),
                           bank=0 if c == 0 else None,
                           inc=(s_tn if c == C - 1 else None))
            tok_ss = None
            last_tsq = None
            for c in range(C):
                pp = c % 2
                last_tsq = EM.emit(
                    "scalar",
                    lambda e, c=c, pp=pp, par=par: e.activation(
                        xsq[pp][:], xtile[par][:, c, :], Act.Square),
                    waits=[t_x, xsq_free[pp]], inc=s_sc)
                tok_ss = mm(pb[5][0:1, :], cones_r[:], xsq[pp][:],
                            start=(c == 0), stop=(c == C - 1), waits=[last_tsq],
                            bank=5 if c == 0 else None, inc=s_tn)
                xsq_free[pp] = tok_ss

            t_row = vchain("vector", s_vc, [
                lambda e: e.tensor_scalar(lnf[0:1, 0, :], pb[0][0:1, :], 1.0 / D, None, Alu.mult),
                lambda e: e.tensor_tensor(lnf[0:1, 1, :], lnf[0:1, 0, :], lnf[0:1, 0, :], Alu.mult),
                lambda e: e.scalar_tensor_tensor(lnf[0:1, 1, :], pb[5][0:1, :], 1.0 / D,
                                                 lnf[0:1, 1, :], Alu.mult, Alu.subtract),
                lambda e: e.tensor_scalar(lnf[0:1, 1, :], lnf[0:1, 1, :], EPS_LN1, None, Alu.add),
            ], waits=[tok_ss, last_lnr_rd])
            read_bank(0, t_row)
            read_bank(5, t_row)
            t_ln = EM.emit(
                "scalar", lambda e: e.activation(lnf[0:1, 1, :], lnf[0:1, 1, :], Act.Ln),
                waits=[t_row], inc=s_sc)
            t_rstd = EM.emit(
                "scalar", lambda e: e.activation(lnr_r[0:1, 0, :], lnf[0:1, 1, :],
                                                 Act.Exp, scale=-0.5),
                waits=[t_ln], inc=s_sc)
            t_lncp = EM.emit(
                "vector",
                lambda e: e.scalar_tensor_tensor(
                    lnr_r[0:1, 1, :], lnf[0:1, 0, :], -1.0, lnr_r[0:1, 0, :],
                    Alu.mult, Alu.mult),
                waits=[t_rstd], inc=s_vc)
            t_A1 = mm(pb[1][:], ones128_r[:], lnr_r[0:1, 0, :], True, True,
                      waits=[t_lncp, t_o128], bank=1, inc=s_tn)
            t_B1 = mm(pb[2][:], ones128_r[:], lnr_r[0:1, 1, :], True, True,
                      bank=2, inc=s_tn)
            last_lnr_rd = t_B1
            last_gmm = None
            t_xnbf = None
            for c in range(C):
                pp = c % 2
                def xn_tt(e, c=c, pp=pp, par=par):
                    e.tensor_tensor(xnf[pp][:], xtile[par][:, c, :], pb[1][:], Alu.mult)
                    return e.tensor_tensor(xnf[pp][:], xnf[pp][:], pb[2][:], Alu.add)
                t_xnf = EM.emit("vector", xn_tt,
                                waits=[t_A1, t_B1, xnf_free[pp], xnf_sc_free[pp]],
                                inc=s_vc)
                t_xnbf = EM.emit(
                    "scalar",
                    lambda e, c=c, pp=pp, t=t: e.copy(
                        xn_bf[:, c, t * TT:(t + 1) * TT], xnf[pp][:]),
                    waits=[t_xnf], inc=s_sc)
                for g in range(NG):
                    last_gmm = mm(
                        pb[3][:, g * 8:(g + 1) * 8],
                        xnf[pp][:, g * 128:(g + 1) * 128],
                        gatet_s[:, c, :],
                        start=(c == 0 and g == 0),
                        stop=(c == C - 1 and g == NG - 1),
                        waits=[t_xnf, t_gatet] if g == 0 else (),
                        bank=3 if (c == 0 and g == 0) else None,
                        inc=(s_tn if g == NG - 1 else None))
                xnf_free[pp] = last_gmm
                xnf_sc_free[pp] = t_xnbf
            xn_done.append(t_xnbf)
            x_free[par] = (t_sum, last_tsq, t_xnf)
            t_sig = EM.emit(
                "scalar",
                lambda e, t=t: e.activation(
                    SCt[:, t * NG:(t + 1) * NG, :].rearrange("p a b -> p (a b)"),
                    pb[3][:, 0:32], Act.Sigmoid),
                waits=[last_gmm], inc=s_sc)
            read_bank(3, t_sig)
            emit_pieces(4)
            read_bank(1, t_xnf)
            read_bank(2, t_xnf)

        rf = [lambda e: e.tensor_tensor(sfc[:], SCt[:], gbias_s[:], Alu.add)]
        for gi in range(2):
            base = gi * 4
            first = True
            for a in range(4):
                for b in range(a + 1, 4):
                    rf.append(lambda e, base=base, a=a, b=b: e.tensor_tensor(
                        rt[2][:], sfc[:, :, base + a], sfc[:, :, base + b], Alu.add))
                    if first:
                        rf.append(lambda e, gi=gi: e.tensor_copy(rt[gi][:], rt[2][:]))
                        first = False
                    else:
                        rf.append(lambda e, gi=gi: e.tensor_tensor(
                            rt[gi][:], rt[gi][:], rt[2][:], Alu.max))
        rf.append(lambda e: e.tensor_tensor(rt[3][:], rt[0][:], rt[1][:], Alu.is_ge))
        rf.append(lambda e: e.tensor_scalar(rt[4][:], rt[3][:], -1.0, 1.0, Alu.mult, Alu.add))
        for x in range(E):
            gm = rt[3] if x < 4 else rt[4]
            rf.append(lambda e, x=x, gm=gm: e.tensor_tensor(
                msk[:, :, x], sfc[:, :, x], gm[:], Alu.mult))
        rf.append(lambda e: e.tensor_tensor(rt[5][:], msk[:, :, 0], msk[:, :, 1], Alu.max))
        for x in range(2, E):
            rf.append(lambda e, x=x: e.tensor_tensor(rt[5][:], rt[5][:], msk[:, :, x], Alu.max))
        for x in range(E):
            rf.append(lambda e, x=x: e.tensor_tensor(
                selw[:, :, x], msk[:, :, x], rt[5][:], Alu.is_ge))
        for x in range(E):
            rf.append(lambda e, x=x: e.scalar_tensor_tensor(
                cw_tm[:, :, x], selw[:, :, x], -1e30, msk[:, :, x], Alu.mult, Alu.add))
        rf.append(lambda e: e.tensor_tensor(rt[6][:], cw_tm[:, :, 0], cw_tm[:, :, 1], Alu.max))
        for x in range(2, E):
            rf.append(lambda e, x=x: e.tensor_tensor(rt[6][:], rt[6][:], cw_tm[:, :, x], Alu.max))
        for x in range(E):
            rf.append(lambda e, x=x: e.tensor_tensor(
                msk[:, :, x], cw_tm[:, :, x], rt[6][:], Alu.is_ge))
            rf.append(lambda e, x=x: e.tensor_tensor(
                selw[:, :, x], selw[:, :, x], msk[:, :, x], Alu.add))
        for x in range(E):
            rf.append(lambda e, x=x: e.tensor_tensor(
                selw[:, :, x], selw[:, :, x], SCt[:, :, x], Alu.mult))
        rf.append(lambda e: e.tensor_tensor(rt[7][:], selw[:, :, 0], selw[:, :, 1], Alu.add))
        for x in range(2, E):
            rf.append(lambda e, x=x: e.tensor_tensor(rt[7][:], rt[7][:], selw[:, :, x], Alu.add))
        rf.append(lambda e: e.tensor_scalar(rt[7][:], rt[7][:], 1e-20, None, Alu.add))
        rf.append(lambda e: e.reciprocal(rt[7][:], rt[7][:]))
        rf.append(lambda e: e.tensor_scalar(rt[7][:], rt[7][:], RSCALE, None, Alu.mult))
        for x in range(E):
            rf.append(lambda e, x=x: e.tensor_tensor(
                cw_tm[:, :, x], selw[:, :, x], rt[7][:], Alu.mult))
        t_rtr = vchain("vector", s_vc, rf, waits=[t_sig, t_gbias])


        s1_ev = {}
        s2_guard = {}
        eb_ctr = [0]
        elu_free = [None, None]
        qk_done = {}
        S_tile_tok = None
        qk_free = None
        t_S_last = None
        t_qk_read = None
        last_s2_mm = None
        t_red_last = None

        ho_B = list(x_free[0]) + list(x_free[1])
        first_B_ev = True
        first_B_tm = True
        first_B_elu = True
        for t in range(NT):
            tsl = slice(t * TT, (t + 1) * TT)
            for proj, wi1, dst1 in (("q", 0, q1_bf), ("k", 1, k1_bf)):
                for mo in range(C):
                    bk = mo % 4
                    wl = [w1_tok[wi1], xn_done[t], t_avr, t_ones]
                    for kc in range(C):
                        mm(pb[bk][:], wslot[wi1][:, kc, mo * 128:(mo + 1) * 128],
                           xn_bf[:, kc, tsl], start=(kc == 0), stop=(kc == C - 1),
                           waits=wl if kc == 0 else (), bank=bk if kc == 0 else None,
                           inc=(s_tn if kc == C - 1 else None))
                    bcol = 0 if proj == "q" else 1
                    tok_b = (s_tn, EM.cnt["s_tn"])
                    wl_ev = [tok_b, t_bcl] + (ho_B if first_B_ev else [])
                    first_B_ev = False
                    if mo % 2 == 0:
                        t_ev = EM.emit(
                            "scalar",
                            lambda e, d=dst1, mo=mo, bk=bk, bcol=bcol: e.activation(
                                d[:, mo, :], pb[bk][:], Act.Identity,
                                bias=bcols_s[:, bcol, mo:mo + 1], scale=1.0),
                            waits=wl_ev, inc=s_sc)
                    else:
                        t_ev = EM.emit(
                            "vector",
                            lambda e, d=dst1, mo=mo, bk=bk, bcol=bcol: e.tensor_scalar(
                                d[:, mo, :], pb[bk][:],
                                bcols_s[:, bcol, mo:mo + 1], None, Alu.add),
                            waits=wl_ev, inc=s_vc)
                    read_bank(bk, t_ev)
                    s1_ev[(proj, t)] = t_ev
            for g in range(NG):
                gi = t * NG + g
                tok_q_done = tok_k_done = None
                for proj, wi2, src1, dst_tm in (
                        ("q", 2, q1_bf, q_tm), ("k", 3, k1_bf, k_tm)):
                    brow = 3 if proj == "q" else 5
                    bks = ((2, 3), (4, 5), (0, 1))[(2 * g + (0 if proj == "q" else 1)) % 3]
                    for half in range(2):
                        bk = bks[half]
                        wl = [w1_tok[wi2], s1_ev[(proj, t)]]
                        for kc in range(C):
                            mm(pb[bk][:],
                               src1[:, kc, g * 128:(g + 1) * 128],
                               wslot[wi2][:, kc, half * 512:(half + 1) * 512],
                               start=(kc == 0), stop=False,
                               waits=wl if kc == 0 else (),
                               bank=bk if kc == 0 else None)
                        tok_b = mm(pb[bk][:], ones128_r[:],
                                   lnr_r[0:1, brow + half, :],
                                   start=False, stop=True, inc=s_tn)
                        last_s2_mm = tok_b
                        eb = eb_ctr[0] % 2
                        eb_ctr[0] += 1
                        wl_min = [tok_b, elu_free[eb]]
                        if first_B_elu:
                            wl_min += [xsq_free[0], xsq_free[1], xnf_free[0],
                                       xnf_free[1], xnf_sc_free[0], xnf_sc_free[1]]
                        t_min = EM.emit(
                            "vector",
                            lambda e, bk=bk, eb=eb: e.tensor_scalar_min(
                                elu_m[eb][:], pb[bk][:], 0.0),
                            waits=wl_min, inc=s_vc)
                        first_B_elu = False
                        t_rel = EM.emit(
                            "scalar",
                            lambda e, bk=bk, eb=eb: e.activation(
                                elu_rt[eb][:], pb[bk][:], Act.Relu),
                            waits=[t_min], inc=s_sc)
                        read_bank(bk, t_rel)
                        t_exp = EM.emit(
                            "scalar",
                            lambda e, eb=eb: e.activation(elu_m[eb][:], elu_m[eb][:], Act.Exp),
                            waits=[t_rel], inc=s_sc)
                        wl_tm = [t_exp, S_tile_tok] + (ho_B if first_B_tm else [])
                        first_B_tm = False
                        t_add = EM.emit(
                            "vector",
                            lambda e, d=dst_tm, g=g, half=half, eb=eb: e.tensor_tensor(
                                d[:, g, half * 512:(half + 1) * 512],
                                elu_rt[eb][:], elu_m[eb][:], Alu.add),
                            waits=wl_tm, inc=s_vc)
                        elu_free[eb] = t_add
                    if proj == "q":
                        tok_q_done = t_add
                    else:
                        tok_k_done = t_add
                for half in range(2):
                    t_qk = EM.emit(
                        "vector",
                        lambda e, g=g, half=half: e.tensor_tensor(
                            qkf[:], q_tm[:, g, half * 512:(half + 1) * 512],
                            k_tm[:, g, half * 512:(half + 1) * 512], Alu.mult),
                        waits=[tok_q_done, tok_k_done, qk_free], inc=s_vc)
                    t_red = EM.emit(
                        "vector",
                        lambda e, gi=gi, half=half: e.tensor_reduce(
                            invq_tm[:, gi * H + half * 8:gi * H + (half + 1) * 8],
                            qkf[:].rearrange("p (h d) -> p h d", d=DH),
                            AxX, Alu.add),
                        waits=[t_qk], inc=s_vc)
                    qk_free = t_red
                    t_red_last = t_red
                qk_done[g] = (tok_q_done, tok_k_done)
                emit_pieces(4)
            for g in range(NG):
                gi = t * NG + g
                for cp in range(C):
                    bk = 6 + cp // 4
                    t_S_last = mm(
                        pb[bk][:, (cp % 4) * 128:(cp % 4 + 1) * 128],
                        k_tm[:, g, cp * 128:(cp + 1) * 128],
                        q_tm[:, g, cp * 128:(cp + 1) * 128],
                        start=(gi == 0 and cp % 4 == 0),
                        stop=(gi == H - 1 and cp % 4 == 3),
                        waits=list(qk_done[g]) if cp == 0 else (),
                        bank=bk if (gi == 0 and cp % 4 == 0) else None,
                        inc=(s_tn if cp == C - 1 else None))
            S_tile_tok = t_S_last
        emit_pieces(len(PIECES))

        t_cwv = None
        for g in range(H):
            bk = 4 + (g % 2)
            t_tr = EM.emit(
                "tensor",
                lambda e, g=g, bk=bk: e.transpose(
                    pb[bk][0:8, 0:128], cw_tm[:, g, :], ident_s[:]),
                waits=[t_rtr, t_ident, bank_guard[bk]], inc=s_tn)
            t_cwv = EM.emit(
                "vector",
                lambda e, g=g, bk=bk: e.tensor_copy(
                    cw_fm[:, g * 128:(g + 1) * 128], pb[bk][0:8, 0:128]),
                waits=[t_tr] + ([elu_free[0], elu_free[1], t_red_last] if g == 0 else []),
                inc=s_vc)
            read_bank(bk, t_cwv)
        t_cwr = EM.emit("scalar", lambda e: e.copy(cw_r[:], cw_fm[:]),
                        waits=[t_cwv], inc=s_sc)

        t_invq = vchain("vector", s_vc, [
            lambda e: e.tensor_scalar(invq_tm[:], invq_tm[:], EPS_QK, None, Alu.add),
            lambda e: e.reciprocal(invq_tm[:], invq_tm[:]),
        ], waits=[t_red_last])
        t_iqv = None
        for gi in range(H):
            bk = 4 + gi % 2
            t_tr = EM.emit(
                "tensor",
                lambda e, gi=gi, bk=bk: e.transpose(
                    pb[bk][0:H, 0:128], invq_tm[:, gi * H:(gi + 1) * H], ident_s[:]),
                waits=[t_invq, t_ident, bank_guard[bk]], inc=s_tn)
            t_iqv = EM.emit(
                "vector",
                lambda e, gi=gi, bk=bk: e.tensor_copy(
                    invq_fm[:, gi * 128:(gi + 1) * 128], pb[bk][0:H, 0:128]),
                waits=[t_tr] + ([t_cwr] if gi == 0 else []), inc=s_vc)
            read_bank(bk, t_iqv)
        t_iqr = EM.emit("scalar", lambda e: e.copy(invq_r[:], invq_fm[:]),
                        waits=[t_iqv, last_s2_mm], inc=s_sc)

        def s_extract(e):
            r = None
            for cp in range(C):
                bk = 6 + cp // 4
                cb = (cp % 4) * 128
                e.tensor_copy(S_pk[0:64, cp * 64:(cp + 1) * 64],
                              pb[bk][0:64, cb:cb + 64])
                r = e.tensor_copy(S_pk[64:128, cp * 64:(cp + 1) * 64],
                                  pb[bk][64:128, cb + 64:cb + 128])
            return r
        t_sx = EM.emit("vector", s_extract, waits=[t_S_last], inc=s_vc)
        read_bank(6, t_sx)
        read_bank(7, t_sx)
        t_sst = dma("scalar", s_scc, cc_in, S_pk[:], waits=[t_sx])
        t_cc = EM.emit(
            "gpsimd",
            lambda e: e.collective_compute(
                "AllReduce", Alu.add,
                replica_groups=[list(g) for g in cc_groups],
                ins=[cc_in], outs=[cc_out]),
            waits=[t_sst], inc=s_cc)
        t_sfl = dma("gpsimd", s_gd, S_fl[:], cc_out, waits=[t_cc])

        handover_D = [last_s2_mm, t_S_last, t_red_last, t_iqr, elu_free[0],
                      elu_free[1], cvt_free[0], cvt_free[1], stg_free[0],
                      stg_free[1]]
        fstr_free = [None] * 8
        frfs_free = [None, None]
        cwb_free = [None, None]
        r2s_free = None
        g_written = []
        last_ffn_mm = None
        last_ffn_v = None

        for t in range(NT):
            tsl = slice(t * TT, (t + 1) * TT)
            r2s_tok = None
            u_tok = None
            for e_i in range(E):
                cb = 4 + e_i % 2
                t_cwb = mm(pb[cb][:],
                           hmask_r[0:8, D + e_i * 128:D + (e_i + 1) * 128],
                           cw_r[:, tsl], True, True,
                           waits=[t_cwr, t_hmr, cwb_free[e_i % 2], bank_guard[cb]],
                           inc=s_tn)
                for hf in range(2):
                    sbuf_i = (e_i * 2 + hf) % 2
                    slab = N_UP0 + e_i * 2 + hf
                    wl = [cache_tok[slab], fstr_free[sbuf_i]]
                    if t == 0 and e_i == 0 and hf == 0:
                        wl += handover_D
                    t_w = dma("sync", s_fs[sbuf_i], fstr[sbuf_i][:], c_w[slab], waits=wl)
                    for ic2 in range(2):
                        icg = hf * 2 + ic2
                        bk = icg % 4
                        for kc in range(C):
                            mm(pb[bk][:],
                               fstr[sbuf_i][:, kc * 256 + ic2 * 128:
                                            kc * 256 + (ic2 + 1) * 128],
                               xn_bf[:, kc, tsl],
                               start=(kc == 0), stop=(kc == C - 1),
                               waits=[t_w, xn_done[t]] if kc == 0 else (),
                               bank=bk if kc == 0 else None,
                               inc=(s_tn if kc == C - 1 else None))
                        tok_up = (s_tn, EM.cnt["s_tn"])
                        pp = icg % 2
                        t_fr = EM.emit(
                            "scalar",
                            lambda e, bk=bk, pp=pp: e.activation(
                                fr_t[pp][:], pb[bk][:], Act.Relu),
                            waits=[tok_up, frfs_free[pp],
                                   (handover_D[0] if (t == 0 and e_i == 0) else None)],
                            inc=s_sc)
                        read_bank(bk, t_fr)
                        t_fs = EM.emit(
                            "scalar",
                            lambda e, pp=pp: e.activation(
                                fs_t[pp][:], fr_t[pp][:], Act.Square),
                            waits=[t_fr], inc=s_sc)
                        slot = e_i * 4 + icg
                        wl2 = [t_fs, t_cwb]
                        if t == 0 and slot == 0:
                            wl2 += handover_D
                        if r2s_free is not None and slot == 0:
                            wl2.append(r2s_free)
                        r2s_tok = EM.emit(
                            "vector",
                            lambda e, slot=slot, cb=cb, pp=pp: e.tensor_tensor(
                                r2s[:, slot, :], fs_t[pp][:], pb[cb][:], Alu.mult),
                            waits=wl2, inc=s_vc)
                        frfs_free[pp] = r2s_tok
                        cwb_free[e_i % 2] = r2s_tok
                        read_bank(cb, r2s_tok)
                    fstr_free[sbuf_i] = tok_up
            for j in range(SIC):
                sbuf_i = 2 + (j // 2) % 2
                if j % 2 == 0:
                    slab = N_SU0 + j // 2
                    wl = [cache_tok[slab], fstr_free[sbuf_i]]
                    if t == 0 and j == 0:
                        wl += handover_D
                    t_su = dma("sync", s_fs[sbuf_i], fstr[sbuf_i][:], c_w[slab], waits=wl)
                bk = j % 4
                for kc in range(C):
                    mm(pb[bk][:],
                       fstr[sbuf_i][:, (j % 2) * 1024 + kc * 128:
                                    (j % 2) * 1024 + (kc + 1) * 128],
                       xn_bf[:, kc, tsl],
                       start=(kc == 0), stop=(kc == C - 1),
                       waits=[t_su, xn_done[t]] if kc == 0 else (),
                       bank=bk if kc == 0 else None,
                       inc=(s_tn if kc == C - 1 else None))
                tok_su = (s_tn, EM.cnt["s_tn"])
                pp = j % 2
                t_fr = EM.emit(
                    "scalar",
                    lambda e, bk=bk, pp=pp: e.activation(
                        fr_t[pp][:], pb[bk][:], Act.Relu),
                    waits=[tok_su, frfs_free[pp]], inc=s_sc)
                read_bank(bk, t_fr)
                wl2 = [t_fr]
                if r2s_free is not None and t > 0 and j == 0:
                    wl2.append(r2s_free)
                u_tok = EM.emit(
                    "vector",
                    lambda e, j=j, pp=pp: e.tensor_tensor(
                        u_bf[:, j, :], fr_t[pp][:], fr_t[pp][:], Alu.mult),
                    waits=wl2, inc=s_vc)
                frfs_free[pp] = u_tok
                if j % 2 == 1:
                    fstr_free[sbuf_i] = tok_su
            t_sd_prev = {}
            for j in range(C):
                sd_i = 4 + j % 2
                slab_sd = N_SD0 + j
                wl = [cache_tok[slab_sd], fstr_free[sd_i]]
                t_sd = dma("sync", s_fs[sd_i], fstr[sd_i][:], c_w[slab_sd], waits=wl)
                bk = 4 + j % 2
                first = True
                for sic in range(SIC):
                    mm(pb[bk][:],
                       fstr[sd_i][:, sic * 128:(sic + 1) * 128],
                       u_bf[:, sic, :],
                       start=first, stop=False,
                       waits=[t_sd, u_tok] if first else (),
                       bank=bk if first else None)
                    first = False
                for eh in range(2):
                    dn_i = 6 + eh
                    slab_dn = N_DN0 + j * 2 + eh
                    wl = [cache_tok[slab_dn], fstr_free[dn_i]]
                    t_dn = dma("sync", s_fs[dn_i], fstr[dn_i][:], c_w[slab_dn], waits=wl)
                    for e4 in range(4):
                        for ic in range(4):
                            is_last = (eh == 1 and e4 == 3 and ic == 3)
                            is_last_eh = (e4 == 3 and ic == 3)
                            last_ffn_mm = mm(
                                pb[bk][:],
                                fstr[dn_i][:, e4 * 512 + ic * 128:
                                           e4 * 512 + (ic + 1) * 128],
                                r2s[:, (eh * 4 + e4) * 4 + ic, :],
                                start=False, stop=is_last,
                                waits=[t_dn, r2s_tok] if (e4 == 0 and ic == 0) else (),
                                inc=(s_tn if is_last_eh else None))
                    fstr_free[dn_i] = last_ffn_mm
                fstr_free[sd_i] = last_ffn_mm
                wl_g = [last_ffn_mm]
                if t == 0 and j == 0:
                    wl_g += [cvt_free[0], cvt_free[1], t_iqr]
                t_gw = EM.emit(
                    "scalar",
                    lambda e, j=j, bk=bk, tsl=tsl: e.copy(
                        g_bf[:, j, tsl], pb[bk][:]),
                    waits=wl_g, inc=s_sc)
                read_bank(bk, t_gw)
                g_written.append(t_gw)
            r2s_free = last_ffn_mm

        bds_fns = [lambda e: e.memset(BDS[:], 0.0)]
        for cp in range(C):
            bds_fns.append(lambda e, cp=cp: e.tensor_copy(
                BDS[0:64, cp, 0:64], S_fl[0:64, cp * 64:(cp + 1) * 64]))
            bds_fns.append(lambda e, cp=cp: e.tensor_copy(
                BDS[64:128, cp, 64:128], S_fl[64:128, cp * 64:(cp + 1) * 64]))
        t_bds = vchain("vector", s_vc, bds_fns, waits=[t_sfl])

        wst_free = [None, None]
        ysq_free = [None, None]
        xE_free = [None, None]
        v1_guard = r2s_free
        vpp_free = [None, None]
        t_out_last = None

        def emit_combine(ctx, step):
            if step == 0:
                ctx["t_A2"] = mm(pb[2][:], ones128_r[:], lnr_r[0:1, 0, :],
                                 True, True,
                                 waits=[ctx["t_lncp2"], bank_guard[2]], inc=s_tn)
                ctx["t_B2"] = mm(pb[3][:], ones128_r[:], lnr_r[0:1, 1, :],
                                 True, True, waits=[bank_guard[3]], inc=s_tn)
                ctx["last_lnr"] = ctx["t_B2"]
            mo = step
            t_sil = EM.emit(
                "scalar",
                lambda e, mo=mo, tsl=ctx["tsl"]: e.activation(
                    s_gs[:], g_bf[:, mo, tsl], Act.Silu),
                waits=[g_written[-1], ctx.get("t_cmb")], inc=s_sc)

            def s_chain(e, mo=mo, par=ctx["par"]):
                e.tensor_tensor(s_t1[:], y_t[:, mo, :], pb[2][:], Alu.mult)
                e.tensor_tensor(s_t1[:], s_t1[:], pb[3][:], Alu.add)
                e.tensor_scalar(s_t1[:], s_t1[:], bcols_s[:, 5, mo:mo + 1],
                                bcols_s[:, 6, mo:mo + 1], Alu.mult, Alu.add)
                e.tensor_tensor(s_t1[:], s_t1[:], s_gs[:], Alu.mult)
                return e.tensor_tensor(
                    xtile[par][:, mo, :], xtile[par][:, mo, :], s_t1[:], Alu.add)
            ctx["t_cmb"] = EM.emit(
                "vector", s_chain,
                waits=[ctx["t_A2"], ctx["t_B2"], t_sil, ctx["t_x2"], t_bcl],
                inc=s_vc)
            if step == C - 1:
                read_bank(2, ctx["t_cmb"])
                read_bank(3, ctx["t_cmb"])
                t_out = dma("scalar", s_ob[ctx["par"]],
                            out_d[:, :, ctx["tsl"]], xtile[ctx["par"]][:],
                            waits=[ctx["t_cmb"]])
                xE_free[ctx["par"]] = t_out
                ctx["t_out"] = t_out

        pend = None
        last_ctx = None
        for t in range(NT):
            par = t % 2
            tsl = slice(t * TT, (t + 1) * TT)
            t_v1 = None
            for mo in range(C):
                wi = (mo // 2) % 2
                if mo % 2 == 0:
                    slab = N_WV0 + mo // 2
                    wl = [cache_tok[slab], wst_free[wi]]
                    if t == 0 and mo == 0:
                        wl += [v1_guard, frfs_free[0], frfs_free[1]]
                    t_wv = dma("sync", s_ws[wi], wst[wi][:], c_w[slab], waits=wl)
                    if mo == 2:
                        wlx = [xE_free[par]]
                        if t < 2:
                            wlx += [last_ffn_mm]
                        t_x2 = dma("sync", s_x[par], xtile[par][:],
                                   di["x"][:, :, tsl], waits=wlx)
                bk = (0, 1, 4, 5)[mo % 4]
                for kc in range(C):
                    mm(pb[bk][:],
                       wst[wi][:, kc * 256 + (mo % 2) * 128:
                               kc * 256 + (mo % 2 + 1) * 128],
                       xn_bf[:, kc, tsl],
                       start=(kc == 0), stop=(kc == C - 1),
                       waits=[t_wv, xn_done[t]] if kc == 0 else (),
                       bank=bk if kc == 0 else None,
                       inc=(s_tn if kc == C - 1 else None))
                tok_b = (s_tn, EM.cnt["s_tn"])
                wl2 = [tok_b, t_bcl]
                if t == 0 and mo == 0:
                    wl2.append(v1_guard)
                if mo % 2 == 0:
                    t_v1 = EM.emit(
                        "scalar",
                        lambda e, mo=mo, bk=bk: e.activation(
                            v1_bf[:, mo, :], pb[bk][:], Act.Identity,
                            bias=bcols_s[:, 2, mo:mo + 1], scale=1.0),
                        waits=wl2, inc=s_sc)
                else:
                    t_v1 = EM.emit(
                        "vector",
                        lambda e, mo=mo, bk=bk: e.tensor_scalar(
                            v1_bf[:, mo, :], pb[bk][:],
                            bcols_s[:, 2, mo:mo + 1], None, Alu.add),
                        waits=wl2, inc=s_vc)
                read_bank(bk, t_v1)
                if mo % 2 == 1:
                    wst_free[wi] = tok_b
                if pend is not None:
                    emit_combine(pend, mo)
            if pend is not None:
                last_lnr_rd = pend["last_lnr"]
                last_ctx = pend
                pend = None
            t_mix = None
            for cch in range(C):
                wi = (cch // 2) % 2
                if cch % 2 == 0:
                    slab = N_WV20 + cch // 2
                    t_wv2 = dma("sync", s_ws[wi], wst[wi][:], c_w[slab],
                                waits=[cache_tok[slab], wst_free[wi]])
                bk = (0, 1, 4, 5)[cch % 4]
                for kc in range(C):
                    mm(pb[bk][:],
                       wst[wi][:, kc * 256 + (cch % 2) * 128:
                               kc * 256 + (cch % 2 + 1) * 128],
                       v1_bf[:, kc, :],
                       start=(kc == 0), stop=(kc == C - 1),
                       waits=[t_wv2, t_v1] if kc == 0 else (),
                       bank=bk if kc == 0 else None,
                       inc=(s_tn if kc == C - 1 else None))
                tok_b = (s_tn, EM.cnt["s_tn"])
                bx = 2 + cch % 2
                tok_x = mm(pb[bx][:], hmask_r[:, cch * 128:(cch + 1) * 128],
                           invq_r[:, tsl], True, True,
                           waits=[t_iqr, t_hmr, bank_guard[bx]], inc=s_tn)
                pp = cch % 2
                wl_qx = [tok_x, vpp_free[pp]]
                if t == 0 and cch < 2:
                    wl_qx.append(v1_guard)
                t_qx = EM.emit(
                    "scalar",
                    lambda e, pp=pp, bx=bx: e.copy(iqx[pp][:], pb[bx][:]),
                    waits=wl_qx, inc=s_sc)
                read_bank(bx, t_qx)
                t_vpp = EM.emit(
                    "vector",
                    lambda e, pp=pp, bk=bk, cch=cch: e.scalar_tensor_tensor(
                        v_pp[pp][:], pb[bk][:], bcols_s[:, 3, cch:cch + 1],
                        iqx[pp][:], Alu.add, Alu.mult),
                    waits=[tok_b, t_qx, t_bcl, vpp_free[pp]], inc=s_vc)
                read_bank(bk, t_vpp)
                bs = 6 + cch % 2
                tok_vs = mm(pb[bs][:], BDS[:, cch, :], v_pp[pp][:], True, True,
                            waits=[t_bds, t_vpp, bank_guard[bs]], inc=s_tn)
                vpp_free[pp] = tok_vs
                t_mix = EM.emit(
                    "scalar",
                    lambda e, cch=cch, bs=bs: e.copy(mix_bf[:, cch, :], pb[bs][:]),
                    waits=[tok_vs], inc=s_sc)
                read_bank(bs, t_mix)
                if cch % 2 == 1:
                    wst_free[wi] = tok_b
            t_y = None
            tok_ss = None
            for mo in range(C):
                wi = (mo // 2) % 2
                if mo % 2 == 0:
                    slab = N_WO0 + mo // 2
                    t_wo = dma("sync", s_ws[wi], wst[wi][:], c_w[slab],
                               waits=[cache_tok[slab], wst_free[wi]])
                bk = (0, 1, 2, 3)[mo % 4]
                for kc in range(C):
                    mm(pb[bk][:],
                       wst[wi][:, kc * 256 + (mo % 2) * 128:
                               kc * 256 + (mo % 2 + 1) * 128],
                       mix_bf[:, kc, :],
                       start=(kc == 0), stop=(kc == C - 1),
                       waits=[t_wo, t_mix] if kc == 0 else (),
                       bank=bk if kc == 0 else None,
                       inc=(s_tn if kc == C - 1 else None))
                tok_b = (s_tn, EM.cnt["s_tn"])
                wl2 = [tok_b, t_bcl]
                if t == 0 and mo == 0:
                    wl2 += [v1_guard]
                t_y = EM.emit(
                    "vector",
                    lambda e, mo=mo, bk=bk, tsl=tsl: e.scalar_tensor_tensor(
                        y_t[:, mo, :], pb[bk][:], bcols_s[:, 4, mo:mo + 1],
                        xn_bf[:, mo, tsl], Alu.add, Alu.add),
                    waits=wl2, inc=s_vc)
                read_bank(bk, t_y)
                if mo % 2 == 1:
                    wst_free[wi] = tok_b
                mm(pb[4][0:1, :], cones_r[:], y_t[:, mo, :],
                   start=(mo == 0), stop=(mo == C - 1),
                   waits=[t_y], bank=4 if mo == 0 else None)
                pp = mo % 2
                t_ysq = EM.emit(
                    "scalar",
                    lambda e, mo=mo, pp=pp: e.activation(
                        ysq[pp][:], y_t[:, mo, :], Act.Square),
                    waits=[t_y, ysq_free[pp]], inc=s_sc)
                tok_ss = mm(pb[5][0:1, :], cones_r[:], ysq[pp][:],
                            start=(mo == 0), stop=(mo == C - 1), waits=[t_ysq],
                            bank=5 if mo == 0 else None, inc=s_tn)
                ysq_free[pp] = tok_ss
            t_row2 = vchain("vector", s_vc, [
                lambda e: e.tensor_scalar(lnf[0:1, 0, :], pb[4][0:1, :], 1.0 / D, None, Alu.mult),
                lambda e: e.tensor_tensor(lnf[0:1, 1, :], lnf[0:1, 0, :], lnf[0:1, 0, :], Alu.mult),
                lambda e: e.scalar_tensor_tensor(lnf[0:1, 1, :], pb[5][0:1, :], 1.0 / D,
                                                 lnf[0:1, 1, :], Alu.mult, Alu.subtract),
                lambda e: e.tensor_scalar(lnf[0:1, 1, :], lnf[0:1, 1, :], EPS_LN2, None, Alu.add),
            ], waits=[tok_ss, last_lnr_rd])
            read_bank(4, t_row2)
            read_bank(5, t_row2)
            t_ln2 = EM.emit(
                "scalar", lambda e: e.activation(lnf[0:1, 1, :], lnf[0:1, 1, :], Act.Ln),
                waits=[t_row2], inc=s_sc)
            t_rstd2 = EM.emit(
                "scalar", lambda e: e.activation(lnr_r[0:1, 0, :], lnf[0:1, 1, :],
                                                 Act.Exp, scale=-0.5),
                waits=[t_ln2], inc=s_sc)
            t_lncp2 = EM.emit(
                "vector",
                lambda e: e.scalar_tensor_tensor(
                    lnr_r[0:1, 1, :], lnf[0:1, 0, :], -1.0, lnr_r[0:1, 0, :],
                    Alu.mult, Alu.mult),
                waits=[t_rstd2], inc=s_vc)
            pend = {"par": par, "tsl": tsl, "t_lncp2": t_lncp2, "t_x2": t_x2}
        for mo in range(C):
            emit_combine(pend, mo)
        last_lnr_rd = pend["last_lnr"]
        t_out_last = pend["t_out"]

        if debug:
            t_d = dma("gpsimd", s_gd, dbg["d_sct"], SCt[:], waits=[t_rtr])
            t_d = dma("gpsimd", s_gd, dbg["d_sfc"], sfc[:], waits=[t_rtr])
            t_d = dma("gpsimd", s_gd, dbg["d_selw"], selw[:], waits=[t_rtr])
            t_d = dma("gpsimd", s_gd, dbg["d_cwtm"], cw_tm[:], waits=[t_rtr])
            t_d = dma("gpsimd", s_gd, dbg["d_spk"], S_pk[:], waits=[t_sx])
            t_d = dma("gpsimd", s_gd, dbg["d_sfl"], S_fl[:], waits=[t_sfl])
            t_d = dma("gpsimd", s_gd, dbg["d_xn"], xn_bf[:], waits=[xn_done[-1]])
            t_d = dma("gpsimd", s_gd, dbg["d_g"], g_bf[:], waits=[g_written[-1]])
            t_d = dma("gpsimd", s_gd, dbg["d_iq"], invq_tm[:], waits=[t_invq])
            t_d = dma("gpsimd", s_gd, dbg["d_cw"], cw_r[:].bitcast(F32), waits=[t_cwr])
            t_d = dma("gpsimd", s_gd, dbg["d_v1"], v1_bf[:], waits=[t_v1])
            t_d = dma("gpsimd", s_gd, dbg["d_mix"], mix_bf[:], waits=[t_mix])
            EM.wait("gpsimd", t_d)
        EM.wait("scalar", (s_ob[0], EM.cnt.get("s_ob0", 0)),
                (s_ob[1], EM.cnt.get("s_ob1", 0)))

        block = es.enter_context(nc.Block())

        @block.sync
        def _(sync):
            for fn in EM.prog["sync"]:
                fn(sync)

        @block.scalar
        def _(scalar):
            for fn in EM.prog["scalar"]:
                fn(scalar)

        @block.vector
        def _(vector):
            for fn in EM.prog["vector"]:
                fn(vector)

        @block.tensor
        def _(tensor):
            for fn in EM.prog["tensor"]:
                fn(tensor)

        @block.gpsimd
        def _(gpsimd):
            for fn in EM.prog["gpsimd"]:
                fn(gpsimd)

    return nc


_CACHE = {}


def _pack_fm(w):
    return np.ascontiguousarray(w.reshape(C, 128, -1).transpose(1, 0, 2))


def _slabs_dout(w):
    a = w.reshape(C, 128, 1024).transpose(1, 0, 2)
    return [np.ascontiguousarray(a[:, :, q * 256:(q + 1) * 256].reshape(128, 2048))
            for q in range(4)]


def _build_inputs(inp):
    x = np.asarray(inp["L0_attn_x"], dtype=np.float32)
    B, L, _ = x.shape
    assert (B, L) == (2, 8192)

    w1 = np.empty((16, 128, 2048), np.float32)
    for wi, nm in enumerate(("Wq", "Wk", "Wq2", "Wk2")):
        a = np.asarray(inp[f"L0_attn_{nm}"], np.float32)
        flat = a.reshape(C, 128, 1024).transpose(1, 0, 2).reshape(128, 8192)
        for q in range(4):
            w1[wi * 4 + q] = flat[:, q * 2048:(q + 1) * 2048]

    wffn = np.empty((N_WSLAB, 128, 2048), np.float32)
    up = np.asarray(inp["L0_ffn_up_weights"], np.float32)
    dn = np.asarray(inp["L0_ffn_down_weights"], np.float32)
    su = np.asarray(inp["L0_ffn_shared_up_weight"], np.float32)
    sd = np.asarray(inp["L0_ffn_shared_down_weight"], np.float32)
    for e_i in range(E):
        upT = up[e_i].T
        a = upT.reshape(C, 128, 512).transpose(1, 0, 2)
        for hf in range(2):
            wffn[N_UP0 + e_i * 2 + hf] = a[:, :, hf * 256:(hf + 1) * 256].reshape(128, 2048)
    for j in range(C):
        for eh in range(2):
            cols = []
            for e4 in range(4):
                dnT = dn[eh * 4 + e4].T
                blk = dnT.reshape(4, 128, 1024)[:, :, j * 128:(j + 1) * 128]
                cols.append(blk.transpose(1, 0, 2).reshape(128, 512))
            wffn[N_DN0 + j * 2 + eh] = np.concatenate(cols, axis=1)
    suT = su.T
    for j2 in range(E):
        cols = []
        for j in (2 * j2, 2 * j2 + 1):
            blk = suT.reshape(C, 128, 2048)[:, :, j * 128:(j + 1) * 128]
            cols.append(blk.transpose(1, 0, 2).reshape(128, 1024))
        wffn[N_SU0 + j2] = np.concatenate(cols, axis=1)
    sdT = sd.T
    for j in range(C):
        blk = sdT.reshape(SIC, 128, 1024)[:, :, j * 128:(j + 1) * 128]
        wffn[N_SD0 + j] = blk.transpose(1, 0, 2).reshape(128, 2048)
    for base, nm in ((N_WV0, "Wv"), (N_WV20, "Wv2"), (N_WO0, "Wo")):
        for q, slab in enumerate(_slabs_dout(np.asarray(inp[f"L0_attn_{nm}"], np.float32))):
            wffn[base + q] = slab

    brows = np.zeros((1, 7, 512), np.float32)
    brows[0, 3] = np.asarray(inp["L0_attn_bq2"], np.float32)[0:512]
    brows[0, 4] = np.asarray(inp["L0_attn_bq2"], np.float32)[512:1024]
    brows[0, 5] = np.asarray(inp["L0_attn_bk2"], np.float32)[0:512]
    brows[0, 6] = np.asarray(inp["L0_attn_bk2"], np.float32)[512:1024]
    bcols = np.zeros((128, 7, C), np.float32)
    for bi, nm in enumerate(("bq", "bk", "bv", "bv2", "bo", "gamma", "beta")):
        bcols[:, bi, :] = np.asarray(
            inp[f"L0_attn_{nm}"], np.float32).reshape(C, 128).T
    gatet = _pack_fm(np.ascontiguousarray(
        np.asarray(inp["L0_ffn_gate_weight"], np.float32).T))
    gbias = np.ascontiguousarray(np.broadcast_to(
        np.asarray(inp["L0_ffn_gate_bias"], np.float32), (128, H, E)))
    hmask = np.zeros((H, 2 * D), np.float32)
    for h in range(H):
        hmask[h, h * DH:(h + 1) * DH] = 1.0
    for e_i in range(E):
        hmask[e_i, D + e_i * 128:D + (e_i + 1) * 128] = 1.0

    shared = {
        "w1": w1, "wffn": wffn, "brows": brows, "bcols": bcols,
        "gatet": gatet, "gbias": gbias,
        "ident": np.eye(128, dtype=np.float32),
        "hmask": hmask,
    }
    x_fm = x.transpose(0, 2, 1)
    in_maps = []
    for core in range(8):
        b = core // 4
        sl = x_fm[b, :, (core % 4) * TC:(core % 4 + 1) * TC]
        xc = np.ascontiguousarray(sl.reshape(C, 128, TC).transpose(1, 0, 2))
        m = dict(shared)
        m["x"] = xc
        in_maps.append(m)
    return in_maps


def kernel(**inputs):
    if "nc" not in _CACHE:
        _CACHE["nc"] = build_program()
    nc = _CACHE["nc"]
    in_maps = _build_inputs(inputs)
    res = run_bass_kernel_spmd(nc, in_maps, core_ids=list(range(8)))
    out = np.empty((2, 8192, D), np.float32)
    for core in range(8):
        oc = np.asarray(res.results[core]["out"])
        fm = oc.transpose(1, 0, 2).reshape(D, TC)
        b = core // 4
        t0 = (core % 4) * TC
        out[b, t0:t0 + TC, :] = fm.T
    return out
